# version 16
# baseline (speedup 1.0000x reference)
import numpy as np
import ml_dtypes
from contextlib import ExitStack

import concourse.bass as bass
import concourse.mybir as mybir
from concourse.bass_utils import run_bass_kernel_spmd

F32 = mybir.dt.float32
BF16 = mybir.dt.bfloat16
I32 = mybir.dt.int32
AF = mybir.ActivationFunctionType
ALU = mybir.AluOpType
AX = mybir.AxisListType

RSQ = 0.08838834764831845
NEGBIG = -1.0e9
EPS = 1e-5

D = 2048
NK = 16
S = 2048
NTC = 16
MQKV = 48
NHEAD = 16
HFF = 5632
NH = 44
NSEQ = 2


class Prog:

    def __init__(self, nc):
        self.nc = nc
        self.ops = []
        self.deps = []
        self.fixed = []
        self.last_writer = {}
        self.readers = {}
        self.marks = {}
        self.stream_cnt = {}
        self.dma_meta = {}

    def mark(self, name):
        self.marks[name] = len(self.ops)

    def fence(self, engine, *tokens):
        j = None
        for k in range(len(self.ops) - 1, -1, -1):
            if self.ops[k][0] == engine and not self.ops[k][2]:
                j = k
                break
        assert j is not None
        for t in tokens:
            self.readers[t] = []
            self.last_writer[t] = j

    def _dep(self, i, j, dep, fx):
        if j in self.dma_meta:
            sem_key, wv, stream, batch = self.dma_meta[j]
            if batch:
                fx.append(("d:" + stream, 16 * self.stream_cnt[stream]))
            else:
                fx.append((sem_key, wv))
        else:
            dep.add(j)

    def op(self, engine, fn, reads=(), writes=(), dma=False):
        i = len(self.ops)
        dep = set()
        fx = []
        for t in reads:
            j = self.last_writer.get(t)
            if j is not None:
                self._dep(i, j, dep, fx)
        for t in writes:
            j = self.last_writer.get(t)
            if j is not None:
                self._dep(i, j, dep, fx)
            for r in self.readers.get(t, ()):
                self._dep(i, r, dep, fx)
        for t in reads:
            self.readers.setdefault(t, []).append(i)
        for t in writes:
            self.readers[t] = []
            self.last_writer[t] = i
        self.ops.append((engine, fn, dma))
        self.deps.append(dep)
        self.fixed.append(fx)
        return i

    def dma(self, q, out, in_, stream, slot=None, batch=False, depth=2, reads=(), writes=()):
        fn = lambda e, out=out, in_=in_: e.dma_start(out=out, in_=in_)
        gen = self.stream_cnt.get(stream, 0)
        if batch:
            sem_key = "d:" + stream
            wv = None
        else:
            if slot is not None:
                assert gen % depth == slot, (stream, gen, slot)
            sem_key = f"d:{stream}:{gen % depth}"
            wv = 16 * (gen // depth + 1)
        i = self.op(q, fn, reads=reads, writes=writes, dma=True)
        self.stream_cnt[stream] = gen + 1
        self.dma_meta[i] = (sem_key, wv, stream, batch)
        return i

    def emit(self, block, ctx, limit=None):
        nc = self.nc
        n = len(self.ops) if limit is None else limit
        pruned = []
        for i in range(n):
            ei = self.ops[i][0]
            dd = {}
            for j in self.deps[i]:
                ej = self.ops[j][0]
                if ej == "tensor" and ei == "tensor" and not self.ops[i][2]:
                    continue
                if dd.get(ej, -1) < j:
                    dd[ej] = j
            pruned.append(dd)
        needed = set()
        for dd in pruned:
            needed.update(dd.values())
        keys = set()
        for i in range(n):
            keys.add(self.ops[i][0])
            if i in self.dma_meta:
                keys.add(self.dma_meta[i][0])
            for (sk, v) in self.fixed[i]:
                keys.add(sk)
        sems = {}
        for s in sorted(keys):
            sems[s] = ctx.enter_context(nc.semaphore("sm_" + s.replace(":", "_")))
        counter = {s: 0 for s in sems}
        val = [None] * n
        for i in range(n):
            if self.ops[i][2]:
                sk = self.dma_meta[i][0]
                counter[sk] += 16
                val[i] = counter[sk]
            elif i in needed:
                s = self.ops[i][0]
                counter[s] += 1
                val[i] = counter[s]
        for s, c in counter.items():
            assert c < 60000, f"sem {s} count {c}"
        waits = [None] * n
        for i in range(n):
            w = {}
            for s, j in pruned[i].items():
                assert val[j] is not None
                w[s] = max(w.get(s, 0), val[j])
            for (sk, v) in self.fixed[i]:
                w[sk] = max(w.get(sk, 0), min(v, counter.get(sk, v)))
            waits[i] = sorted(w.items())

        def mk(engname):
            def inner(eng):
                seen = {}
                for i in range(n):
                    if self.ops[i][0] != engname:
                        continue
                    for (s, v) in waits[i]:
                        if seen.get(s, 0) < v:
                            eng.wait_ge(sems[s], v)
                            seen[s] = v
                    ins = self.ops[i][1](eng)
                    if val[i] is not None:
                        sk = self.dma_meta[i][0] if self.ops[i][2] else self.ops[i][0]
                        ins.then_inc(sems[sk], 16 if self.ops[i][2] else 1)
            return inner

        block.sync(mk("sync"))
        block.gpsimd(mk("gpsimd"))

        def mkv(eng):
            mk("vector")(eng)
            for s in sorted(sems):
                if s.startswith("d:") and counter[s] > 0:
                    eng.wait_ge(sems[s], counter[s])
        block.vector(mkv)
        block.scalar(mk("scalar"))
        block.tensor(mk("tensor"))


def build_nc(stop=None):
    nc = bass.Bass("TRN2", target_bir_lowering=False, debug=False)

    xin = nc.dram_tensor("xin", [NSEQ, S, D], F32, kind="ExternalInput").ap()
    wqkv = nc.dram_tensor("wqkv", [D, 6144], F32, kind="ExternalInput").ap()
    bqkv = nc.dram_tensor("bqkv", [6144], F32, kind="ExternalInput").ap()
    wo = nc.dram_tensor("wo", [D, D], F32, kind="ExternalInput").ap()
    bo = nc.dram_tensor("bo", [D], F32, kind="ExternalInput").ap()
    wqg = nc.dram_tensor("wqg", [HFF, D], I32, kind="ExternalInput").ap()
    sg = nc.dram_tensor("sg", [HFF], F32, kind="ExternalInput").ap()
    wqu = nc.dram_tensor("wqu", [HFF, D], I32, kind="ExternalInput").ap()
    su = nc.dram_tensor("su", [HFF], F32, kind="ExternalInput").ap()
    wqd = nc.dram_tensor("wqd", [D, HFF], I32, kind="ExternalInput").ap()
    sd = nc.dram_tensor("sd", [D], F32, kind="ExternalInput").ap()
    out = nc.dram_tensor("out", [NSEQ, S, D], F32, kind="ExternalOutput").ap()

    Qs = nc.dram_tensor("Qs", [NHEAD, 128, S], BF16, kind="Internal").ap()
    Vs = nc.dram_tensor("Vs", [NHEAD, 128, S], BF16, kind="Internal").ap()
    X2 = nc.dram_tensor("X2", [S, D], F32, kind="Internal").ap()
    Gc = nc.dram_tensor("Gc", [NH, 128, NK, 128], BF16, kind="Internal").ap()
    Uc = nc.dram_tensor("Uc", [NH, 128, NK, 128], BF16, kind="Internal").ap()
    Dc = nc.dram_tensor("Dc", [NK, 128, NH, 128], BF16, kind="Internal").ap()
    Wqc = nc.dram_tensor("Wqc", [MQKV, 128, NK, 128], BF16, kind="Internal").ap()
    Woc = nc.dram_tensor("Woc", [NK, 128, NK, 128], BF16, kind="Internal").ap()
    RawG = nc.dram_tensor("RawG", [HFF, D], BF16, kind="Internal").ap()
    RawU = nc.dram_tensor("RawU", [HFF, D], BF16, kind="Internal").ap()
    RawD = nc.dram_tensor("RawD", [D, HFF], BF16, kind="Internal").ap()

    bfc = lambda a: a.astype(ml_dtypes.bfloat16)
    ident_bf_c = nc.inline_tensor(bfc(np.eye(128, dtype=np.float32)), "idbf").ap()
    ident_f_c = nc.inline_tensor(np.eye(128, dtype=np.float32), "idf").ap()
    mask_c = nc.inline_tensor(np.triu(np.full((128, 128), NEGBIG, dtype=np.float32), 1), "mask").ap()
    ones_c = nc.inline_tensor(np.ones((1, 128), dtype=ml_dtypes.bfloat16), "ones").ap()
    eps_c = nc.inline_tensor(np.full((128, 1), EPS, dtype=np.float32), "eps").ap()
    n128_c = nc.inline_tensor(np.full((1, 128), -128.0, dtype=ml_dtypes.bfloat16), "n128").ap()
    csh_c = nc.inline_tensor(np.full((128, 1), -48.0, dtype=np.float32), "csh").ap()
    onescol_c = nc.inline_tensor(np.ones((128, 1), dtype=ml_dtypes.bfloat16), "onescl").ap()

    S0 = 16512

    def at(name, off_b, shape, dt):
        assert off_b % 32 == 0, (name, off_b)
        return nc.alloc_sbuf_tensor_at(name, shape, dt, offset=off_b)

    RB1 = S0
    RB2 = S0 + 65536
    RC = S0 + 131072
    RD = S0 + 135168
    R4 = S0 + 157696

    ln1_fm = at("ln1fm", RB1, [128, NK, S], BF16)
    att_sb = at("attsb", RB1, [128, NHEAD, 512], BF16)
    h_fm = at("hfm", RB1, [128, NH, 512], BF16)
    ln2_fm = at("ln2fm", RB1 + NH * 1024, [128, NK, 512], BF16)

    K_sb = at("ksb", RB2, [128, NHEAD, S], BF16)
    gc_sb = [at(f"gcs{i}", RB2 + i * 4096, [128, NK, 128], BF16) for i in range(2)]
    uc_sb = [at(f"ucs{i}", RB2 + 8192 + i * 4096, [128, NK, 128], BF16) for i in range(2)]
    dc_sb = [at(f"dcs{i}", RB2 + 16384 + i * 11264, [128, NH, 128], BF16) for i in range(2)]

    PT_sb = at("ptsb", RC, [128, NK, 128], BF16)
    lnst = at("lnst", RC, [128, S], BF16)
    P_sb = [at("psb0", RD, [128, S], BF16), None]
    V_sb = [at(f"vsb{i}", RD + 4096 + i * 4096, [128, NK, 128], BF16) for i in range(2)]
    Q_sb = [at(f"qsb{i}", RD + 12288 + i * 1024, [128, 512], BF16) for i in range(2)]
    wo_sb = [at(f"wos{i}", RD + 14336 + i * 4096, [128, NK, 128], BF16) for i in range(2)]
    wq_sb = [at(f"wqs{i}", RD + i * 4096, [128, NK, 128], BF16) for i in range(2)]
    pfm = [at(f"pfm{i}", RD + i * 2048, [128, 512], F32) for i in range(2)]
    silu_t = [at(f"silu{i}", RD + i * 2048, [128, 512], F32) for i in range(2)]
    cvw = [at(f"cvw{i}", R4 + 12288 + i * 2816, [128, 11, 128], BF16) for i in range(2)]
    cvt = [at(f"cvT{i}", R4 + 17920 + i * 2816, [128, 11 * 128], BF16) for i in range(2)]
    s2row = at("s2row", RD + 4096, [1, 512], BF16)
    s3row_f = at("s3rowf", RD + 5120, [1, 512], F32)
    s3hi_bf = at("s3hibf", RD + 7168, [1, 512], BF16)
    s3hi_f = at("s3hif", RD + 8192, [1, 512], F32)
    s3lo_bf = at("s3lobf", RD + 10240, [1, 512], BF16)
    lnstA1 = at("lnstA1", RD + 14336, [128, S], BF16)
    ln2stB1 = at("ln2stB1", RB1 + 61440, [128, S], BF16)
    tm_buf = at("tmbuf", R4, [128, 4, D], F32)
    x_st = [at(f"xst{i}", R4 + 32768 + i * 8192, [128, D], F32) for i in range(2)]
    qstage = [at(f"qstg{i}", R4 + i * 4096, [128, S], BF16) for i in range(2)]
    vstage = [at(f"vstg{i}", R4 + 8192 + i * 1024, [128, 512], BF16) for i in range(2)]
    vtstage = [at(f"vtst{i}", R4 + 10240 + i * 1024, [128, 512], BF16) for i in range(2)]
    SM = R4 + 49152
    ident_bf = at("identbf", SM, [128, 128], BF16)
    ident_f = at("identf", SM + 256, [128, 128], F32)
    mask_sb = at("masksb", SM + 768, [128, 128], F32)
    bq_sb = at("bqsb", SM + 1280, [128, MQKV], F32)
    bo_sb = at("bosb", SM + 1472, [128, NK], F32)
    sg_sb = at("sgsb", SM + 1536, [128, NH], F32)
    su_sb = at("susb", SM + 1728, [128, NH], F32)
    sd_sb = at("sdsb", SM + 1920, [128, NK], F32)
    bstage = at("bstage", SM + 1984, [128, 128], F32)
    eps_sb = at("epssb", SM + 3392, [128, 1], F32)
    csh_sb = at("cshsb", SM + 3744, [128, 1], F32)
    neg128_row = at("n128r", SM + 3424, [1, 128], BF16)
    ones_col = at("onescol", SM + 3680, [128, 1], BF16)
    stat = [at(f"stat{i}", SM + 2496 + i * 96, [128, 24], F32) for i in range(2)]
    mv = [at(f"mv{i}", SM + 2688 + i * 32, [128, 2], F32) for i in range(2)]
    rstd = [at(f"rstd{i}", SM + 2752 + i * 32, [128, 1], F32) for i in range(2)]
    mrstd = [at(f"mrstd{i}", SM + 2816 + i * 32, [128, 1], F32) for i in range(2)]
    veps = [at(f"veps{i}", SM + 2880 + i * 32, [128, 1], F32) for i in range(2)]
    nmax_sb = at("nmaxsb", SM + 2944, [128, 1], F32)
    z_sb = at("zsb", SM + 2976, [128, 1], F32)
    zi_sb = at("zisb", SM + 3008, [128, 1], F32)
    ones_bf = at("onesbf", SM + 3040, [1, 128], BF16)
    nm4 = at("nm4", SM + 3296, [128, 4], F32)
    z4 = at("z4", SM + 3328, [128, 8], F32)
    zi4 = at("zi4", SM + 3360, [128, 4], F32)
    zrow_bf = at("zrowbf", RB1 + 16384, [1, 512], BF16)
    zbc_sb = at("zbcsb", RB1 + 17408, [128, 512], BF16)
    P_sb[1] = at("psb1", RB1 + 18432, [128, S], BF16)
    tmW = at("tmw", RB1 + 22528, [128, 4, D], F32)

    with ExitStack() as ctx:
        psA = ctx.enter_context(nc.psum_tensor("psA", [128, 2048], F32))
        psB = ctx.enter_context(nc.psum_tensor("psB", [128, 1024], F32))
        psC = ctx.enter_context(nc.psum_tensor("psC", [128, 2048], BF16))
        block = ctx.enter_context(nc.Block())

        p = Prog(nc)
        PBA = [f"pb{i}" for i in range(4)]
        PBB = [f"pb{i + 4}" for i in range(2)]
        PBC = [f"pb{i + 6}" for i in range(2)]

        p.dma("sync", ident_bf[:], ident_bf_c, stream="init", batch=True, writes=("idbf",))
        p.dma("sync", ident_f[:], ident_f_c, stream="init", batch=True, writes=("idf",))
        p.dma("sync", mask_sb[:], mask_c, stream="init", batch=True, writes=("mask",))
        p.dma("sync", ones_bf[:], ones_c, stream="init", batch=True, writes=("ones",))
        nc.const_aps.register(F32, EPS, eps_sb[:])
        p.dma("sync", eps_sb[:], eps_c, stream="init", batch=True, writes=("epsc",))
        p.dma("sync", neg128_row[:], n128_c, stream="init", batch=True, writes=("n128",))
        p.dma("sync", csh_sb[:], csh_c, stream="init", batch=True, writes=("csh",))
        p.dma("sync", ones_col[:], onescol_c, stream="init", batch=True, writes=("onescol",))

        def vec_prep(dram_vec, nchunk, dst, dst_tok):
            p.dma("sync", bstage[0:nchunk, :], dram_vec.rearrange("(c p) -> c p", p=128),
                  stream="bst", depth=1, reads=("bstage",), writes=("bstage",))
            p.op("tensor", lambda e, n=nchunk: e.transpose(psB[:, 0:n], bstage[0:n, 0:128], ident_f[0:n, 0:n]),
                 reads=("bstage", "idf", PBB[0]), writes=(PBB[0],))
            p.op("vector", lambda e, n=nchunk, dst=dst: e.tensor_copy(dst[:, 0:n], psB[:, 0:n]),
                 reads=(PBB[0],), writes=(dst_tok, PBB[0]))

        vec_prep(bqkv, MQKV, bq_sb, "bq")
        vec_prep(bo, NK, bo_sb, "bo")
        vec_prep(sg, NH, sg_sb, "sg")
        vec_prep(su, NH, su_sb, "su")
        vec_prep(sd, NK, sd_sb, "sd")
        p.op("vector", lambda e: e.tensor_scalar(bq_sb[:, 0:16], bq_sb[:, 0:16], RSQ, None, ALU.mult),
             reads=("bq",), writes=("bq",))

        p.mark("prep")
        build_steps = []
        for i in range(4):
            build_steps.append(("raw", RawG[i * 1408:(i + 1) * 1408, :], wqg[i * 1408:(i + 1) * 1408, :], "rawg", "rawgdone"))
            build_steps.append(("raw", RawU[i * 1408:(i + 1) * 1408, :], wqu[i * 1408:(i + 1) * 1408, :], "rawu", "rawudone"))
            build_steps.append(("raw", RawD[i * 512:(i + 1) * 512, :], wqd[i * 512:(i + 1) * 512, :], "rawd", "rawddone"))
        for st in range(MQKV):
            build_steps.append(("strip", Wqc, wqkv, st, f"wqc{st // 8}", f"wqcd{st // 8}"))
        for st in range(NK):
            build_steps.append(("strip", Woc, wo, st, f"woc{st // 8}", f"wocd{st // 8}"))
        build_pos = [0]

        def run_builds(n):
            for _ in range(n):
                if build_pos[0] < len(build_steps):
                    step = build_steps[build_pos[0]]
                    build_pos[0] += 1
                    if step[0] == "raw":
                        _, dst, src, strm, tok = step
                        p.dma("gpsimd", dst, src, stream=strm, batch=True, writes=(tok,))
                    else:
                        _, cache, src, st, strm, tok = step
                        p.dma("gpsimd", cache[st], src[:, st * 128:(st + 1) * 128].rearrange("(k p) m -> p k m", p=128),
                              stream=strm, batch=True, writes=(tok,))

        pending_dma = []

        def flush_pending():
            for fn in pending_dma:
                fn()
            pending_dma.clear()

        conv_steps = []

        def conv_load(idx, extra=()):
            wq_ap, cache4, k, r0, rr_n, tag = conv_steps[idx]
            par = idx % 2
            p.dma("sync" if par == 0 else "gpsimd", cvw[par][:, 0:rr_n, :],
                  wq_ap[r0 * 128:(r0 + rr_n) * 128, k * 128:(k + 1) * 128].rearrange("(r p) c -> p r c", p=128),
                  stream="cvi", slot=par, reads=(f"cvw{par}", f"raw{tag}done") + tuple(extra), writes=(f"cvw{par}",))

        def conv_xpose(idx):
            wq_ap, cache4, k, r0, rr_n, tag = conv_steps[idx]
            par = idx % 2
            nev = (rr_n + 3) // 4
            for b in range(nev):
                b0, b1 = b * 4, min(b * 4 + 4, rr_n)
                bank = b % 2
                for rr in range(b0, b1):
                    p.op("tensor", lambda e, rr=rr, bank=bank, par=par: e.transpose(
                        psC[:, bank * 1024 + (rr % 8) * 128: bank * 1024 + (rr % 8) * 128 + 128],
                        cvw[par][:, rr, :], ident_bf[:]),
                        reads=(f"cvw{par}", "idbf", PBC[bank]), writes=(PBC[bank],))
                p.op("vector", lambda e, par=par, b0=b0, b1=b1, bank=bank: e.tensor_copy(
                    cvt[par][:, b0 * 128:b1 * 128],
                    psC[:, bank * 1024 + (b0 % 8) * 128: bank * 1024 + (b0 % 8) * 128 + (b1 - b0) * 128]),
                    reads=(PBC[bank], f"cvt{par}"), writes=(f"cvt{par}", PBC[bank]))
            p.dma("gpsimd", cache4[r0:r0 + rr_n, :, k, :].rearrange("r p m -> p r m"),
                  cvt[par][:, 0:rr_n * 128].rearrange("p (r m) -> p r m", m=128),
                  stream=f"{tag}cache", batch=True, reads=(f"cvt{par}",), writes=(f"{tag}done",))

        for wq_ap, cache4, nk, nr, tag in ((RawG, Gc, NK, NH, "g"), (RawU, Uc, NK, NH, "u"), (RawD, Dc, NH, NK, "d")):
            for k in range(nk):
                for r0 in range(0, nr, 11):
                    rr_n = min(11, nr - r0)
                    conv_steps.append((wq_ap, cache4, k, r0, rr_n, tag))
        conv_lp = [0]
        conv_xp = [0]

        def run_conv(n, extra=()):
            for _ in range(n):
                while conv_lp[0] < len(conv_steps) and conv_lp[0] < conv_xp[0] + 2:
                    conv_load(conv_lp[0], extra)
                    conv_lp[0] += 1
                if conv_xp[0] < conv_lp[0]:
                    conv_xpose(conv_xp[0])
                    conv_xp[0] += 1

        CVD = ("gdone", "udone", "ddone")

        p.mark("conv")
        def ln_rstd(src_ap, src_toks, par):
            for i in range(4):
                p.op("vector", lambda e, i=i, par=par, src_ap=src_ap: e.bn_stats(
                    stat[par][:, i * 6:(i + 1) * 6], src_ap[:, i * 512:(i + 1) * 512]),
                    reads=src_toks, writes=(f"stat{par}",))
            p.op("vector", lambda e, par=par: e.bn_aggr(mv[par][:], stat[par][:]),
                 reads=(f"stat{par}",), writes=(f"mv{par}",))
            p.op("scalar", lambda e, par=par: e.activation(rstd[par][:], mv[par][:, 1:2], AF.Ln, bias=EPS),
                 reads=(f"mv{par}", "epsc"), writes=(f"rstd{par}",))
            p.op("scalar", lambda e, par=par: e.activation(rstd[par][:], rstd[par][:], AF.Exp, scale=-0.5),
                 reads=(f"rstd{par}",), writes=(f"rstd{par}",))

        def transpose_to_fm(src_bf_ap, src_toks, n_tiles, dst_fn, dst_tokens, extra=()):
            nev = (n_tiles + 3) // 4
            for b in range(nev):
                r0, r1 = b * 4, min(b * 4 + 4, n_tiles)
                bank = b % 2
                for rr in range(r0, r1):
                    p.op("tensor", lambda e, rr=rr, bank=bank, src_bf_ap=src_bf_ap: e.transpose(
                        psC[:, bank * 1024 + (rr % 8) * 128: bank * 1024 + (rr % 8) * 128 + 128],
                        src_bf_ap[:, rr * 128:(rr + 1) * 128], ident_bf[:]),
                        reads=tuple(src_toks) + ("idbf", PBC[bank]), writes=(PBC[bank],))
                dst_ap = dst_fn(r0, r1)
                p.op("scalar", lambda e, dst_ap=dst_ap, r0=r0, r1=r1, bank=bank: e.activation(
                    dst_ap, psC[:, bank * 1024 + (r0 % 8) * 128: bank * 1024 + (r0 % 8) * 128 + (r1 - r0) * 128]
                    .rearrange("p (k t) -> p k t", t=128),
                    AF.Identity, scale=1.0),
                    reads=(PBC[bank],) + tuple(extra), writes=tuple(dst_tokens) + (PBC[bank],))

        for seq in range(NSEQ):
            pef = () if seq == 0 else (f"PEF_s{seq-1}",)
            pout = () if seq == 0 else (f"OUT_{seq-1}_15",)
            lnpair = (lnst, lnstA1)
            for tc in range(NTC):
                par = tc % 2
                p.dma("sync", x_st[par][:], xin[seq, tc * 128:(tc + 1) * 128, :],
                      stream="xst", slot=par, reads=(f"xst{par}",), writes=(f"xst{par}",))
                ln_rstd(x_st[par][:], (f"xst{par}",), par)
                p.op("vector", lambda e, par=par: e.tensor_scalar(
                    lnpair[par][:], x_st[par][:], mv[par][:, 0:1], rstd[par][:], ALU.subtract, ALU.mult),
                    reads=(f"xst{par}", f"rstd{par}", f"mv{par}", f"lnp{par}"), writes=(f"lnp{par}",))
                transpose_to_fm(lnpair[par][:], (f"lnp{par}",), 16,
                                lambda r0, r1, tc=tc: ln1_fm[:, r0:r1, tc * 128:(tc + 1) * 128],
                                (f"l1f_{tc}",))
                if seq == 0:
                    run_builds(5)
            p.fence("tensor", f"LN1F_s{seq}")
            p.mark(f"s{seq}_ln1")
            for m in range(MQKV):
                par = m % 2
                p.dma("sync", wq_sb[par][:], Wqc[m],
                      stream="wq", slot=par, reads=(f"wqs{par}", f"wqcd{m // 8}") + pef, writes=(f"wqs{par}",))
                if seq == 0:
                    run_conv(2)
                for tt in range(4):
                    bank = (m * 4 + tt) % 4
                    tts = tuple(f"l1f_{tt * 4 + i}" for i in range(4))
                    for k in range(NK):
                        p.op("tensor", lambda e, k=k, tt=tt, par=par, bank=bank: e.matmul(
                            psA[:, bank * 512:(bank + 1) * 512],
                            wq_sb[par][:, k, :], ln1_fm[:, k, tt * 512:(tt + 1) * 512],
                            start=(k == 0), stop=(k == NK - 1)),
                            reads=(f"wqs{par}",) + tts + (PBA[bank],), writes=(PBA[bank],))
                    if m < 16:
                        p.op("scalar", lambda e, m=m, tt=tt, bank=bank, par=par: e.activation(
                            qstage[par][:, tt * 512:(tt + 1) * 512], psA[:, bank * 512:(bank + 1) * 512],
                            AF.Identity, bias=bq_sb[:, m:m + 1], scale=RSQ),
                            reads=(PBA[bank], "bq", f"qstg{par}") + pout,
                            writes=(f"qstg{par}_{tt}", f"qstg{par}", PBA[bank]))
                    elif m < 32:
                        h = m - 16
                        p.op("scalar", lambda e, h=h, tt=tt, bank=bank: e.activation(
                            K_sb[:, h, tt * 512:(tt + 1) * 512], psA[:, bank * 512:(bank + 1) * 512],
                            AF.Identity, bias=bq_sb[:, 16 + h:17 + h], scale=1.0),
                            reads=(PBA[bank], "bq"), writes=(f"K_{h}_{tt}", PBA[bank]))
                    else:
                        h = m - 32
                        vpar = tt % 2
                        p.op("scalar", lambda e, h=h, bank=bank, vpar=vpar: e.activation(
                            vstage[vpar][:], psA[:, bank * 512:(bank + 1) * 512],
                            AF.Identity, bias=bq_sb[:, 32 + h:33 + h], scale=1.0),
                            reads=(PBA[bank], "bq", f"vstg{vpar}") + pout, writes=(f"vstg{vpar}", PBA[bank]))
                        transpose_to_fm(vstage[vpar][:], (f"vstg{vpar}",), 4,
                                        lambda r0, r1, vpar=vpar: vtstage[vpar][:, r0 * 128:r1 * 128].rearrange("p (k t) -> p k t", t=128),
                                        (f"vtst{vpar}",), extra=pout)
                        p.dma("gpsimd", Vs[h, :, tt * 512:(tt + 1) * 512], vtstage[vpar][:],
                              stream="vspill", batch=True, reads=(f"vtst{vpar}",), writes=(f"Vs_{h}_{tt}",))
                if m < 16:
                    p.dma("gpsimd", Qs[m, :, :], qstage[par][:],
                          stream="qspill", batch=True,
                          reads=(f"qstg{par}_0", f"qstg{par}_1", f"qstg{par}_2", f"qstg{par}_3"),
                          writes=(f"Qs_{m}",))
            p.fence("tensor", f"QKVF_s{seq}")
            p.mark(f"s{seq}_qkv")
            for g in range(4):
                def wo_load(m, seq=seq):
                    par = m % 2
                    p.dma("sync", wo_sb[par][:], Woc[m],
                          stream="wos", slot=par,
                          reads=(f"wos{par}", f"wocd{m // 8}", f"LN1F_s{seq}") + CVD, writes=(f"wos{par}",))
                wo_load(0)
                wo_load(1)

                def vq_load(h, g=g, seq=seq):
                    vpar = h % 2
                    p.dma("sync", V_sb[vpar][:, 0:(g + 1) * 4, :],
                          Vs[h, :, 0:(g + 1) * 512].rearrange("p (c t) -> p c t", t=128),
                          stream="vsb", slot=vpar,
                          reads=(f"vsb{vpar}", f"QKVF_s{seq}") + CVD + tuple(f"Vs_{h}_{t}" for t in range(g + 1)),
                          writes=(f"vsb{vpar}",))
                    p.dma("sync", Q_sb[vpar][:], Qs[h, :, g * 512:(g + 1) * 512],
                          stream="qsb", slot=vpar, reads=(f"qsb{vpar}", f"Qs_{h}") + CVD, writes=(f"qsb{vpar}",))

                vq_load(0)
                for h in range(NHEAD):
                    vpar = h % 2
                    if h == 2:
                        flush_pending()
                    if h + 1 < NHEAD:
                        vq_load(h + 1)
                    if seq == 0:
                        run_conv(2)
                    pvb = PBB[g % 2]
                    pvo = PBB[1 - g % 2]

                    def pt_pv(ql, vpar=vpar, g=g):
                        qt = g * 4 + ql
                        pp = ql % 2
                        nsc = (qt + 4) // 4
                        for sc in range(nsc):
                            r0, r1 = sc * 4, min(sc * 4 + 4, qt + 1)
                            bank = sc % 2
                            for rr in range(r0, r1):
                                p.op("tensor", lambda e, rr=rr, bank=bank, pp=pp: e.transpose(
                                    psC[:, bank * 1024 + (rr % 8) * 128: bank * 1024 + (rr % 8) * 128 + 128],
                                    P_sb[pp][:, rr * 128:(rr + 1) * 128], ident_bf[:]),
                                    reads=(f"P{pp}_{sc}", "idbf", PBC[bank]), writes=(PBC[bank],))
                            eng = "vector" if (sc % 2 == 0) else "scalar"
                            psrc = psC[:, bank * 1024 + (r0 % 8) * 128: bank * 1024 + (r0 % 8) * 128 + (r1 - r0) * 128]
                            if eng == "vector":
                                fn = (lambda e, r0=r0, r1=r1, psrc=psrc: e.tensor_copy(
                                    PT_sb[:, r0:r1, :], psrc.rearrange("p (c t) -> p c t", t=128)))
                            else:
                                fn = (lambda e, r0=r0, r1=r1, psrc=psrc: e.activation(
                                    PT_sb[:, r0:r1, :], psrc.rearrange("p (c t) -> p c t", t=128),
                                    AF.Identity, scale=1.0))
                            p.op(eng, fn, reads=(PBC[bank], f"PTb{sc}"), writes=(f"PTb{sc}", PBC[bank]))
                        p.op("vector", lambda e, nsc=nsc, pp=pp: e.reduce_sum(
                            z_sb[:], z4[:, 4 * pp:4 * pp + nsc], axis=AX.X),
                            reads=(f"z4{pp}",), writes=("z",))
                        p.op("vector", lambda e, ql=ql: e.reciprocal(zi4[:, ql:ql + 1], z_sb[:]),
                             reads=("z", "zi4"), writes=("zi4",))
                        for sc2 in range(qt + 1):
                            p.op("tensor", lambda e, sc2=sc2, vpar=vpar, ql=ql, g=g, qt=qt: e.matmul(
                                psB[:, (g % 2) * 512 + ql * 128:(g % 2) * 512 + (ql + 1) * 128],
                                V_sb[vpar][:, sc2, :], PT_sb[:, sc2, :],
                                start=(sc2 == 0), stop=(sc2 == qt)),
                                reads=(f"vsb{vpar}", f"PTb{sc2 // 4}", pvb), writes=(pvb,))

                    pend = None
                    for ql in range(4):
                        qt = g * 4 + ql
                        L = (qt + 1) * 128
                        nsc = (L + 511) // 512
                        pp = ql % 2
                        for sc in range(nsc):
                            wdt = min(512, L - sc * 512)
                            p.op("tensor", lambda e, h=h, sc=sc, wdt=wdt, vpar=vpar, ql=ql: e.matmul(
                                psA[:, sc * 512: sc * 512 + wdt],
                                Q_sb[vpar][:, ql * 128:(ql + 1) * 128], K_sb[:, h, sc * 512: sc * 512 + wdt],
                                start=True, stop=True),
                                reads=(f"qsb{vpar}", f"K_{h}_{sc}", PBA[sc]), writes=(PBA[sc],))
                            if sc == nsc - 1:
                                p.op("vector", lambda e, L=L: e.tensor_add(psA[:, L - 128:L], psA[:, L - 128:L], mask_sb[:]),
                                     reads=("mask", PBA[sc]), writes=(PBA[sc],))
                            p.op("scalar", lambda e, sc=sc, wdt=wdt, pp=pp: e.activation(
                                P_sb[pp][:, sc * 512: sc * 512 + wdt], psA[:, sc * 512: sc * 512 + wdt], AF.Exp,
                                bias=csh_sb[:], scale=1.0, accum_out=z4[:, 4 * pp + sc:4 * pp + sc + 1]),
                                reads=(PBA[sc], "csh", f"P{pp}_{sc}", f"z4{pp}") + CVD,
                                writes=(f"P{pp}_{sc}", f"z4{pp}", PBA[sc]))
                        if pend is not None:
                            pt_pv(pend)
                        pend = ql
                    pt_pv(pend)
                    for ql in range(4):
                        p.op("tensor", lambda e, ql=ql, g=g: e.transpose(
                            psB[0:1, (1 - g % 2) * 512 + ql * 128:(1 - g % 2) * 512 + (ql + 1) * 128],
                            zi4[:, ql:ql + 1], ident_f[:]),
                            reads=("zi4", "idf", pvo), writes=(pvo,))
                    p.op("vector", lambda e, g=g: e.tensor_copy(
                        zrow_bf[:], psB[0:1, (1 - g % 2) * 512:(1 - g % 2) * 512 + 512]),
                        reads=(pvo, "zrow"), writes=("zrow", pvo))
                    p.op("tensor", lambda e, g=g: e.matmul(
                        psB[:, (1 - g % 2) * 512:(1 - g % 2) * 512 + 512], ones_bf[:], zrow_bf[:],
                        start=True, stop=True),
                        reads=("ones", "zrow", pvo), writes=(pvo,))
                    p.op("vector", lambda e, g=g: e.tensor_copy(
                        zbc_sb[:], psB[:, (1 - g % 2) * 512:(1 - g % 2) * 512 + 512]),
                        reads=(pvo, "zbc"), writes=("zbc", pvo))
                    p.op("vector", lambda e, h=h, g=g: e.tensor_tensor(
                        att_sb[:, h, :], psB[:, (g % 2) * 512:(g % 2) * 512 + 512], zbc_sb[:], ALU.mult),
                        reads=(pvb, "zbc"), writes=(f"att_{h}", pvb))
                for m in range(NK):
                    par = m % 2
                    bank = PBA[m % 2]
                    pend_wo = m + 2 if m + 2 < NK else None
                    for k in range(NK):
                        p.op("tensor", lambda e, k=k, par=par, m=m: e.matmul(
                            psA[:, (m % 2) * 512:(m % 2) * 512 + 512], wo_sb[par][:, k, :], att_sb[:, k, :],
                            start=(k == 0), stop=(k == NK - 1)),
                            reads=(f"wos{par}", f"att_{k}", bank), writes=(bank,))
                    if pend_wo is not None:
                        wo_load(pend_wo)
                    p.op("scalar", lambda e, m=m, par=par: e.activation(
                        pfm[par][:], psA[:, (m % 2) * 512:(m % 2) * 512 + 512], AF.Identity,
                        bias=bo_sb[:, m:m + 1], scale=1.0),
                        reads=(bank, "bo", f"pfm{par}") + CVD, writes=(f"pfm{par}", bank))
                    tb = PBB[m % 2]
                    for tcl in range(4):
                        p.op("tensor", lambda e, tcl=tcl, m=m, par=par: e.transpose(
                            psB[:, (m % 2) * 512 + tcl * 128: (m % 2) * 512 + (tcl + 1) * 128],
                            pfm[par][:, tcl * 128:(tcl + 1) * 128], ident_f[:]),
                            reads=(f"pfm{par}", "idf", tb), writes=(tb,))
                    p.op("vector", lambda e, m=m: e.tensor_copy(
                        tmW[:, :, m * 128:(m + 1) * 128],
                        psB[:, (m % 2) * 512:(m % 2) * 512 + 512].rearrange("p (c t) -> p c t", t=128)),
                        reads=(tb, f"tm_{m}"), writes=(f"tm_{m}", tb))
                tmtoks = tuple(f"tm_{m}" for m in range(NK))
                for tcl in range(4):
                    tc = g * 4 + tcl
                    par = tcl % 2
                    p.dma("sync", x_st[par][:], xin[seq, tc * 128:(tc + 1) * 128, :],
                          stream="xst", slot=par, reads=(f"xst{par}",), writes=(f"xst{par}",))
                    srcv = tmW[:, tcl, :]
                    ln_rstd(srcv, tmtoks, par)
                    p.op("vector", lambda e, srcv=srcv, par=par: e.tensor_scalar(
                        srcv, srcv, mv[par][:, 0:1], rstd[par][:], ALU.subtract, ALU.mult),
                        reads=tmtoks + (f"rstd{par}", f"mv{par}"), writes=tmtoks)
                    p.op("vector", lambda e, srcv=srcv, par=par: e.tensor_add(srcv, srcv, x_st[par][:]),
                         reads=(f"xst{par}",) + tmtoks, writes=tmtoks)
                    pending_dma.append(lambda tc=tc, srcv=srcv, tmtoks=tmtoks: p.dma(
                        "gpsimd", X2[tc * 128:(tc + 1) * 128, :], srcv,
                        stream="x2w", batch=True, reads=tmtoks, writes=(f"X2_{tc}",)))
            p.fence("tensor", f"KF_s{seq}")
            p.mark(f"s{seq}_p1d")
            flush_pending()
            if seq == 0:
                run_conv(len(conv_steps))
            ln2pair = (lnst, ln2stB1)

            def emit_ln2(g):
                for tcl in range(4):
                    tc = g * 4 + tcl
                    par = tcl % 2
                    p.dma("sync", x_st[par][:], X2[tc * 128:(tc + 1) * 128, :],
                          stream="xst", slot=par, reads=(f"xst{par}", f"X2_{tc}"), writes=(f"xst{par}",))
                    ln_rstd(x_st[par][:], (f"xst{par}",), par)
                    p.op("vector", lambda e, par=par: e.tensor_scalar(
                        ln2pair[par][:], x_st[par][:], mv[par][:, 0:1], rstd[par][:], ALU.subtract, ALU.mult),
                        reads=(f"xst{par}", f"rstd{par}", f"mv{par}", f"l2p{par}"), writes=(f"l2p{par}",))
                    transpose_to_fm(ln2pair[par][:], (f"l2p{par}",), 16,
                                    lambda r0, r1, tcl=tcl: ln2_fm[:, r0:r1, tcl * 128:(tcl + 1) * 128],
                                    (f"l2f_{tcl}",), extra=("X2_15",))
                l2t = tuple(f"l2f_{t}" for t in range(4))
                for k in range(NK):
                    p.op("tensor", lambda e, k=k: e.matmul(
                        psB[0:1, 0:512], ones_col[:], ln2_fm[:, k, :],
                        start=(k == 0), stop=(k == NK - 1)),
                        reads=("onescol", PBB[0]) + l2t, writes=(PBB[0],))
                p.op("scalar", lambda e: e.activation(s2row[:], psB[0:1, 0:512], AF.Identity, scale=1.0),
                     reads=(PBB[0], "s2row"), writes=("s2row", PBB[0]))

            emit_ln2(0)
            for g in range(4):
                l2toks = tuple(f"l2f_{t}" for t in range(4))
                for r in range(NH):
                    par = r % 2
                    if r == 2:
                        flush_pending()
                    p.dma("sync", gc_sb[par][:], Gc[r],
                          stream="gc", slot=par, reads=(f"gcs{par}", "gdone", f"KF_s{seq}"), writes=(f"gcs{par}",))
                    p.dma("sync", uc_sb[par][:], Uc[r],
                          stream="uc", slot=par, reads=(f"ucs{par}", "udone", f"KF_s{seq}"), writes=(f"ucs{par}",))
                    gb = PBA[r % 2]
                    ub = PBA[2 + r % 2]
                    for k in range(NK):
                        p.op("tensor", lambda e, k=k, par=par, r=r: e.matmul(
                            psA[:, (r % 2) * 512:(r % 2) * 512 + 512], gc_sb[par][:, k, :], ln2_fm[:, k, :],
                            start=(k == 0), stop=False),
                            reads=(f"gcs{par}",) + l2toks + (gb,), writes=(gb,))
                    p.op("tensor", lambda e, r=r: e.matmul(
                        psA[:, (r % 2) * 512:(r % 2) * 512 + 512], neg128_row[:], s2row[:],
                        start=False, stop=True),
                        reads=("n128", "s2row", gb), writes=(gb,))
                    for k in range(NK):
                        p.op("tensor", lambda e, k=k, par=par, r=r: e.matmul(
                            psA[:, (2 + r % 2) * 512:(2 + r % 2) * 512 + 512], uc_sb[par][:, k, :], ln2_fm[:, k, :],
                            start=(k == 0), stop=False),
                            reads=(f"ucs{par}",) + l2toks + (ub,), writes=(ub,))
                    p.op("tensor", lambda e, r=r: e.matmul(
                        psA[:, (2 + r % 2) * 512:(2 + r % 2) * 512 + 512], neg128_row[:], s2row[:],
                        start=False, stop=True),
                        reads=("n128", "s2row", ub), writes=(ub,))
                    p.op("scalar", lambda e, r=r, par=par: e.activation(
                        silu_t[par][:], psA[:, (r % 2) * 512:(r % 2) * 512 + 512], AF.Silu, scale=sg_sb[:, r:r + 1]),
                        reads=(gb, "sg", f"silu{par}") + CVD, writes=(f"silu{par}", gb))
                    p.op("vector", lambda e, r=r, par=par: e.scalar_tensor_tensor(
                        h_fm[:, r, :], silu_t[par][:], su_sb[:, r:r + 1],
                        psA[:, (2 + r % 2) * 512:(2 + r % 2) * 512 + 512], ALU.mult, ALU.mult),
                        reads=(f"silu{par}", "su", ub, "X2_15"), writes=(f"h_{r}", ub))
                htoks = tuple(f"h_{r}" for r in range(NH))
                for kh in range(NH):
                    p.op("tensor", lambda e, kh=kh: e.matmul(
                        psB[0:1, 512:1024], ones_col[:], h_fm[:, kh, :],
                        start=(kh == 0), stop=(kh == NH - 1)),
                        reads=("onescol", PBB[1]) + htoks, writes=(PBB[1],))
                p.op("scalar", lambda e: e.activation(s3row_f[:], psB[0:1, 512:1024], AF.Identity, scale=1.0),
                     reads=(PBB[1], "s3rowf"), writes=("s3rowf", PBB[1]))
                p.op("vector", lambda e: e.tensor_copy(s3hi_bf[:], s3row_f[:]),
                     reads=("s3rowf", "s3hibf"), writes=("s3hibf",))
                p.op("vector", lambda e: e.tensor_copy(s3hi_f[:], s3hi_bf[:]),
                     reads=("s3hibf", "s3hif"), writes=("s3hif",))
                p.op("vector", lambda e: e.tensor_sub(s3lo_bf[:], s3row_f[:], s3hi_f[:]),
                     reads=("s3rowf", "s3hif", "s3lobf"), writes=("s3lobf",))
                if g < 3:
                    emit_ln2(g + 1)
                for rd in range(NK):
                    par = rd % 2
                    p.dma("sync", dc_sb[par][:], Dc[rd],
                          stream="dc", slot=par, reads=(f"dcs{par}", "ddone", f"KF_s{seq}"), writes=(f"dcs{par}",))
                    db = PBA[rd % 2]
                    for kh in range(NH):
                        p.op("tensor", lambda e, kh=kh, par=par, rd=rd: e.matmul(
                            psA[:, (rd % 2) * 512:(rd % 2) * 512 + 512], dc_sb[par][:, kh, :], h_fm[:, kh, :],
                            start=(kh == 0), stop=False),
                            reads=(f"dcs{par}",) + htoks + (db,), writes=(db,))
                    p.op("tensor", lambda e, rd=rd: e.matmul(
                        psA[:, (rd % 2) * 512:(rd % 2) * 512 + 512], neg128_row[:], s3hi_bf[:],
                        start=False, stop=False),
                        reads=("n128", "s3hibf", db), writes=(db,))
                    p.op("tensor", lambda e, rd=rd: e.matmul(
                        psA[:, (rd % 2) * 512:(rd % 2) * 512 + 512], neg128_row[:], s3lo_bf[:],
                        start=False, stop=True),
                        reads=("n128", "s3lobf", db), writes=(db,))
                    p.op("vector", lambda e, rd=rd, par=par: e.tensor_scalar(
                        pfm[par][:], psA[:, (rd % 2) * 512:(rd % 2) * 512 + 512], sd_sb[:, rd:rd + 1], None, ALU.mult),
                        reads=(db, "sd", f"pfm{par}"), writes=(f"pfm{par}", db))
                    tb = PBB[rd % 2]
                    for tcl in range(4):
                        p.op("tensor", lambda e, tcl=tcl, rd=rd, par=par: e.transpose(
                            psB[:, (rd % 2) * 512 + tcl * 128: (rd % 2) * 512 + (tcl + 1) * 128],
                            pfm[par][:, tcl * 128:(tcl + 1) * 128], ident_f[:]),
                            reads=(f"pfm{par}", "idf", tb), writes=(tb,))
                    p.op("vector", lambda e, rd=rd: e.tensor_copy(
                        tm_buf[:, :, rd * 128:(rd + 1) * 128],
                        psB[:, (rd % 2) * 512:(rd % 2) * 512 + 512].rearrange("p (c t) -> p c t", t=128)),
                        reads=(tb, f"tf_{rd}"), writes=(f"tf_{rd}", tb, "cvw0", "cvw1", "cvt0", "cvt1"))
                tmtoks = tuple(f"tf_{m}" for m in range(NK))
                for tcl in range(4):
                    tc = g * 4 + tcl
                    par = tcl % 2
                    p.dma("sync", x_st[par][:], X2[tc * 128:(tc + 1) * 128, :],
                          stream="xst", slot=par, reads=(f"xst{par}", f"X2_{tc}"), writes=(f"xst{par}",))
                    srcv = tm_buf[:, tcl, :]
                    ln_rstd(srcv, tmtoks, par)
                    p.op("vector", lambda e, srcv=srcv, par=par: e.tensor_scalar(
                        srcv, srcv, mv[par][:, 0:1], rstd[par][:], ALU.subtract, ALU.mult),
                        reads=tmtoks + (f"rstd{par}", f"mv{par}"), writes=tmtoks)
                    p.op("vector", lambda e, srcv=srcv, par=par: e.tensor_add(srcv, srcv, x_st[par][:]),
                         reads=(f"xst{par}",) + tmtoks, writes=tmtoks)
                    pending_dma.append(lambda tc=tc, srcv=srcv, tmtoks=tmtoks, seq=seq: p.dma(
                        "gpsimd", out[seq, tc * 128:(tc + 1) * 128, :], srcv,
                        stream="outw", batch=True, reads=tmtoks, writes=(f"OUT_{seq}_{tc}",)))
            flush_pending()
            p.fence("tensor", f"PEF_s{seq}")
            p.mark(f"s{seq}_ffn")

        p.op("vector", lambda e: e.drain(),
             reads=tuple(f"OUT_{s}_{t}" for s in range(NSEQ) for t in range(NTC)))

        p.emit(block, ctx, p.marks.get(stop) if stop else None)
    return nc


_CACHE = {}


def kernel(**inputs):
    x = np.ascontiguousarray(np.asarray(inputs["L0_attn_x"], dtype=np.float32))
    wq = np.ascontiguousarray(np.asarray(inputs["L0_attn_Wqkv"], dtype=np.float32))
    bq = np.ascontiguousarray(np.asarray(inputs["L0_attn_bqkv"], dtype=np.float32))
    wo_ = np.ascontiguousarray(np.asarray(inputs["L0_attn_Wo"], dtype=np.float32))
    bo_ = np.ascontiguousarray(np.asarray(inputs["L0_attn_bo"], dtype=np.float32))
    wg = np.ascontiguousarray(np.asarray(inputs["L0_ffn_wq_gate"], dtype=np.int32))
    sg_ = np.ascontiguousarray(np.asarray(inputs["L0_ffn_s_gate"], dtype=np.float32))
    wu = np.ascontiguousarray(np.asarray(inputs["L0_ffn_wq_up"], dtype=np.int32))
    su_ = np.ascontiguousarray(np.asarray(inputs["L0_ffn_s_up"], dtype=np.float32))
    wd = np.ascontiguousarray(np.asarray(inputs["L0_ffn_wq_down"], dtype=np.int32))
    sd_ = np.ascontiguousarray(np.asarray(inputs["L0_ffn_s_down"], dtype=np.float32))

    if "nc" not in _CACHE:
        _CACHE["nc"] = build_nc()
    nc = _CACHE["nc"]

    in_maps = []
    for c in range(8):
        in_maps.append({
            "xin": x[2 * c:2 * c + 2],
            "wqkv": wq, "bqkv": bq, "wo": wo_, "bo": bo_,
            "wqg": wg, "sg": sg_, "wqu": wu, "su": su_, "wqd": wd, "sd": sd_,
        })
    res = run_bass_kernel_spmd(nc, in_maps, core_ids=list(range(8)))
    return np.concatenate([r["out"] for r in res.results], axis=0)


# version 17
# speedup vs baseline: 1.1914x; 1.1914x over previous
import numpy as np
import ml_dtypes
from contextlib import ExitStack

import concourse.bass as bass
import concourse.mybir as mybir
from concourse.bass_utils import run_bass_kernel_spmd

F32 = mybir.dt.float32
BF16 = mybir.dt.bfloat16
I32 = mybir.dt.int32
AF = mybir.ActivationFunctionType
ALU = mybir.AluOpType
AX = mybir.AxisListType

RSQ = 0.08838834764831845
NEGBIG = -1.0e9
EPS = 1e-5

D = 2048
NK = 16
S = 2048
NTC = 16
MQKV = 48
NHEAD = 16
HFF = 5632
NH = 44
NSEQ = 2


class Prog:

    def __init__(self, nc):
        self.nc = nc
        self.ops = []
        self.deps = []
        self.fixed = []
        self.last_writer = {}
        self.readers = {}
        self.marks = {}
        self.stream_cnt = {}
        self.dma_meta = {}

    def mark(self, name):
        self.marks[name] = len(self.ops)

    def fence(self, engine, *tokens):
        j = None
        for k in range(len(self.ops) - 1, -1, -1):
            if self.ops[k][0] == engine and not self.ops[k][2]:
                j = k
                break
        assert j is not None
        for t in tokens:
            self.readers[t] = []
            self.last_writer[t] = j

    def _dep(self, i, j, dep, fx):
        if j in self.dma_meta:
            sem_key, wv, stream, batch = self.dma_meta[j]
            if batch:
                fx.append(("d:" + stream, 16 * self.stream_cnt[stream]))
            else:
                fx.append((sem_key, wv))
        else:
            dep.add(j)

    def op(self, engine, fn, reads=(), writes=(), dma=False):
        i = len(self.ops)
        dep = set()
        fx = []
        for t in reads:
            j = self.last_writer.get(t)
            if j is not None:
                self._dep(i, j, dep, fx)
        for t in writes:
            j = self.last_writer.get(t)
            if j is not None:
                self._dep(i, j, dep, fx)
            for r in self.readers.get(t, ()):
                self._dep(i, r, dep, fx)
        for t in reads:
            self.readers.setdefault(t, []).append(i)
        for t in writes:
            self.readers[t] = []
            self.last_writer[t] = i
        self.ops.append((engine, fn, dma))
        self.deps.append(dep)
        self.fixed.append(fx)
        return i

    def dma(self, q, out, in_, stream, slot=None, batch=False, depth=2, reads=(), writes=()):
        fn = lambda e, out=out, in_=in_: e.dma_start(out=out, in_=in_)
        gen = self.stream_cnt.get(stream, 0)
        if batch:
            sem_key = "d:" + stream
            wv = None
        else:
            if slot is not None:
                assert gen % depth == slot, (stream, gen, slot)
            sem_key = f"d:{stream}:{gen % depth}"
            wv = 16 * (gen // depth + 1)
        i = self.op(q, fn, reads=reads, writes=writes, dma=True)
        self.stream_cnt[stream] = gen + 1
        self.dma_meta[i] = (sem_key, wv, stream, batch)
        return i

    def emit(self, block, ctx, limit=None):
        nc = self.nc
        n = len(self.ops) if limit is None else limit
        pruned = []
        for i in range(n):
            ei = self.ops[i][0]
            dd = {}
            for j in self.deps[i]:
                ej = self.ops[j][0]
                if ej == "tensor" and ei == "tensor" and not self.ops[i][2]:
                    continue
                if dd.get(ej, -1) < j:
                    dd[ej] = j
            pruned.append(dd)
        needed = set()
        for dd in pruned:
            needed.update(dd.values())
        keys = set()
        for i in range(n):
            keys.add(self.ops[i][0])
            if i in self.dma_meta:
                keys.add(self.dma_meta[i][0])
            for (sk, v) in self.fixed[i]:
                keys.add(sk)
        sems = {}
        for s in sorted(keys):
            sems[s] = ctx.enter_context(nc.semaphore("sm_" + s.replace(":", "_")))
        counter = {s: 0 for s in sems}
        val = [None] * n
        for i in range(n):
            if self.ops[i][2]:
                sk = self.dma_meta[i][0]
                counter[sk] += 16
                val[i] = counter[sk]
            elif i in needed:
                s = self.ops[i][0]
                counter[s] += 1
                val[i] = counter[s]
        for s, c in counter.items():
            assert c < 60000, f"sem {s} count {c}"
        waits = [None] * n
        for i in range(n):
            w = {}
            for s, j in pruned[i].items():
                assert val[j] is not None
                w[s] = max(w.get(s, 0), val[j])
            for (sk, v) in self.fixed[i]:
                w[sk] = max(w.get(sk, 0), min(v, counter.get(sk, v)))
            waits[i] = sorted(w.items())

        def mk(engname):
            def inner(eng):
                seen = {}
                for i in range(n):
                    if self.ops[i][0] != engname:
                        continue
                    for (s, v) in waits[i]:
                        if seen.get(s, 0) < v:
                            eng.wait_ge(sems[s], v)
                            seen[s] = v
                    ins = self.ops[i][1](eng)
                    if val[i] is not None:
                        sk = self.dma_meta[i][0] if self.ops[i][2] else self.ops[i][0]
                        ins.then_inc(sems[sk], 16 if self.ops[i][2] else 1)
            return inner

        block.sync(mk("sync"))
        block.gpsimd(mk("gpsimd"))

        def mkv(eng):
            mk("vector")(eng)
            for s in sorted(sems):
                if s.startswith("d:") and counter[s] > 0:
                    eng.wait_ge(sems[s], counter[s])
        block.vector(mkv)
        block.scalar(mk("scalar"))
        block.tensor(mk("tensor"))


def build_nc(stop=None):
    nc = bass.Bass("TRN2", target_bir_lowering=False, debug=False)

    xin = nc.dram_tensor("xin", [NSEQ, S, D], F32, kind="ExternalInput").ap()
    wqkv = nc.dram_tensor("wqkv", [D, 6144], F32, kind="ExternalInput").ap()
    bqkv = nc.dram_tensor("bqkv", [6144], F32, kind="ExternalInput").ap()
    wo = nc.dram_tensor("wo", [D, D], F32, kind="ExternalInput").ap()
    bo = nc.dram_tensor("bo", [D], F32, kind="ExternalInput").ap()
    wqg = nc.dram_tensor("wqg", [HFF, D], I32, kind="ExternalInput").ap()
    sg = nc.dram_tensor("sg", [HFF], F32, kind="ExternalInput").ap()
    wqu = nc.dram_tensor("wqu", [HFF, D], I32, kind="ExternalInput").ap()
    su = nc.dram_tensor("su", [HFF], F32, kind="ExternalInput").ap()
    wqd = nc.dram_tensor("wqd", [D, HFF], I32, kind="ExternalInput").ap()
    sd = nc.dram_tensor("sd", [D], F32, kind="ExternalInput").ap()
    out = nc.dram_tensor("out", [NSEQ, S, D], F32, kind="ExternalOutput").ap()

    Qs = nc.dram_tensor("Qs", [NHEAD, 128, S], BF16, kind="Internal").ap()
    Vs = nc.dram_tensor("Vs", [NHEAD, 128, S], BF16, kind="Internal").ap()
    X2 = nc.dram_tensor("X2", [S, D], F32, kind="Internal").ap()
    Gc = nc.dram_tensor("Gc", [NH, 128, NK, 128], BF16, kind="Internal").ap()
    Uc = nc.dram_tensor("Uc", [NH, 128, NK, 128], BF16, kind="Internal").ap()
    Dc = nc.dram_tensor("Dc", [NK, 128, NH, 128], BF16, kind="Internal").ap()
    Wqc = nc.dram_tensor("Wqc", [MQKV, 128, NK, 128], BF16, kind="Internal").ap()
    Woc = nc.dram_tensor("Woc", [NK, 128, NK, 128], BF16, kind="Internal").ap()
    RawG = nc.dram_tensor("RawG", [HFF, D], BF16, kind="Internal").ap()
    RawU = nc.dram_tensor("RawU", [HFF, D], BF16, kind="Internal").ap()
    RawD = nc.dram_tensor("RawD", [D, HFF], BF16, kind="Internal").ap()

    bfc = lambda a: a.astype(ml_dtypes.bfloat16)
    ident_bf_c = nc.inline_tensor(bfc(np.eye(128, dtype=np.float32)), "idbf").ap()
    ident_f_c = nc.inline_tensor(np.eye(128, dtype=np.float32), "idf").ap()
    mask_c = nc.inline_tensor(np.triu(np.full((128, 128), NEGBIG, dtype=np.float32), 1), "mask").ap()
    ones_c = nc.inline_tensor(np.ones((1, 128), dtype=ml_dtypes.bfloat16), "ones").ap()
    eps_c = nc.inline_tensor(np.full((128, 1), EPS, dtype=np.float32), "eps").ap()
    n128_c = nc.inline_tensor(np.full((1, 128), -128.0, dtype=ml_dtypes.bfloat16), "n128").ap()
    csh_c = nc.inline_tensor(np.full((128, 1), -48.0, dtype=np.float32), "csh").ap()
    onescol_c = nc.inline_tensor(np.ones((128, 1), dtype=ml_dtypes.bfloat16), "onescl").ap()

    S0 = 16512

    def at(name, off_b, shape, dt):
        assert off_b % 32 == 0, (name, off_b)
        return nc.alloc_sbuf_tensor_at(name, shape, dt, offset=off_b)

    RB1 = S0
    RB2 = S0 + 65536
    RC = S0 + 131072
    RD = S0 + 135168
    R4 = S0 + 157696

    ln1_fm = at("ln1fm", RB1, [128, NK, S], BF16)
    att_sb = at("attsb", RB1, [128, NHEAD, 512], BF16)
    h_fm = at("hfm", RB1, [128, NH, 512], BF16)
    ln2_fm = at("ln2fm", RB1 + NH * 1024, [128, NK, 512], BF16)

    K_sb = at("ksb", RB2, [128, NHEAD, S], BF16)
    gc_sb = [at(f"gcs{i}", RB2 + i * 4096, [128, NK, 128], BF16) for i in range(2)]
    uc_sb = [at(f"ucs{i}", RB2 + 8192 + i * 4096, [128, NK, 128], BF16) for i in range(2)]
    dc_sb = [at(f"dcs{i}", RB2 + 16384 + i * 11264, [128, NH, 128], BF16) for i in range(2)]

    PT_sb = at("ptsb", RC, [128, NK, 128], BF16)
    lnst = at("lnst", RC, [128, S], BF16)
    P_sb = [at("psb0", RD, [128, S], BF16), None]
    V_sb = [at(f"vsb{i}", RD + 4096 + i * 4096, [128, NK, 128], BF16) for i in range(2)]
    Q_sb = [at(f"qsb{i}", RD + 12288 + i * 1024, [128, 512], BF16) for i in range(2)]
    wo_sb = [at(f"wos{i}", RD + 14336 + i * 4096, [128, NK, 128], BF16) for i in range(2)]
    wq_sb = [at(f"wqs{i}", RD + i * 4096, [128, NK, 128], BF16) for i in range(2)]
    pfm = [at(f"pfm{i}", RD + i * 2048, [128, 512], F32) for i in range(2)]
    silu_t = [at(f"silu{i}", RD + i * 2048, [128, 512], F32) for i in range(2)]
    cvw = [at(f"cvw{i}", R4 + 12288 + i * 2816, [128, 11, 128], BF16) for i in range(2)]
    cvt = [at(f"cvT{i}", R4 + 17920 + i * 2816, [128, 11 * 128], BF16) for i in range(2)]
    s2row = at("s2row", RD + 4096, [1, 512], BF16)
    s3row_f = at("s3rowf", RD + 5120, [1, 512], F32)
    s3hi_bf = at("s3hibf", RD + 7168, [1, 512], BF16)
    s3hi_f = at("s3hif", RD + 8192, [1, 512], F32)
    s3lo_bf = at("s3lobf", RD + 10240, [1, 512], BF16)
    lnstA1 = at("lnstA1", RD + 14336, [128, S], BF16)
    ln2stB1 = at("ln2stB1", RB1 + 61440, [128, S], BF16)
    tm_buf = at("tmbuf", R4, [128, 4, D], F32)
    x_st = [at(f"xst{i}", R4 + 32768 + i * 8192, [128, D], F32) for i in range(2)]
    qstage = [at(f"qstg{i}", R4 + i * 4096, [128, S], BF16) for i in range(2)]
    vstage = [at(f"vstg{i}", R4 + 8192 + i * 1024, [128, 512], BF16) for i in range(2)]
    vtstage = [at(f"vtst{i}", R4 + 10240 + i * 1024, [128, 512], BF16) for i in range(2)]
    SM = R4 + 49152
    ident_bf = at("identbf", SM, [128, 128], BF16)
    ident_f = at("identf", SM + 256, [128, 128], F32)
    mask_sb = at("masksb", SM + 768, [128, 128], F32)
    bq_sb = at("bqsb", SM + 1280, [128, MQKV], F32)
    bo_sb = at("bosb", SM + 1472, [128, NK], F32)
    sg_sb = at("sgsb", SM + 1536, [128, NH], F32)
    su_sb = at("susb", SM + 1728, [128, NH], F32)
    sd_sb = at("sdsb", SM + 1920, [128, NK], F32)
    bstage = at("bstage", SM + 1984, [128, 128], F32)
    eps_sb = at("epssb", SM + 3392, [128, 1], F32)
    csh_sb = at("cshsb", SM + 3744, [128, 1], F32)
    neg128_row = at("n128r", SM + 3424, [1, 128], BF16)
    ones_col = at("onescol", SM + 3680, [128, 1], BF16)
    stat = [at(f"stat{i}", SM + 2496 + i * 96, [128, 24], F32) for i in range(2)]
    mv = [at(f"mv{i}", SM + 2688 + i * 32, [128, 2], F32) for i in range(2)]
    rstd = [at(f"rstd{i}", SM + 2752 + i * 32, [128, 1], F32) for i in range(2)]
    mrstd = [at(f"mrstd{i}", SM + 2816 + i * 32, [128, 1], F32) for i in range(2)]
    veps = [at(f"veps{i}", SM + 2880 + i * 32, [128, 1], F32) for i in range(2)]
    nmax_sb = at("nmaxsb", SM + 2944, [128, 1], F32)
    z_sb = at("zsb", SM + 2976, [128, 1], F32)
    zi_sb = at("zisb", SM + 3008, [128, 1], F32)
    ones_bf = at("onesbf", SM + 3040, [1, 128], BF16)
    nm4 = at("nm4", SM + 3296, [128, 4], F32)
    z4 = at("z4", SM + 3328, [128, 8], F32)
    zi4 = at("zi4", SM + 3360, [128, 4], F32)
    zrow_bf = at("zrowbf", RB1 + 16384, [1, 512], BF16)
    zbc_sb = at("zbcsb", RB1 + 17408, [128, 512], BF16)
    P_sb[1] = at("psb1", RB1 + 18432, [128, S], BF16)
    tmW = at("tmw", RB1 + 22528, [128, 4, D], F32)

    with ExitStack() as ctx:
        psA = ctx.enter_context(nc.psum_tensor("psA", [128, 2048], F32))
        psB = ctx.enter_context(nc.psum_tensor("psB", [128, 1024], F32))
        psC = ctx.enter_context(nc.psum_tensor("psC", [128, 2048], BF16))
        block = ctx.enter_context(nc.Block())

        p = Prog(nc)
        PBA = [f"pb{i}" for i in range(4)]
        PBB = [f"pb{i + 4}" for i in range(2)]
        PBC = [f"pb{i + 6}" for i in range(2)]

        p.dma("sync", ident_bf[:], ident_bf_c, stream="init", batch=True, writes=("idbf",))
        p.dma("sync", ident_f[:], ident_f_c, stream="init", batch=True, writes=("idf",))
        p.dma("sync", mask_sb[:], mask_c, stream="init", batch=True, writes=("mask",))
        p.dma("sync", ones_bf[:], ones_c, stream="init", batch=True, writes=("ones",))
        nc.const_aps.register(F32, EPS, eps_sb[:])
        p.dma("sync", eps_sb[:], eps_c, stream="init", batch=True, writes=("epsc",))
        p.dma("sync", neg128_row[:], n128_c, stream="init", batch=True, writes=("n128",))
        p.dma("sync", csh_sb[:], csh_c, stream="init", batch=True, writes=("csh",))
        p.dma("sync", ones_col[:], onescol_c, stream="init", batch=True, writes=("onescol",))

        def vec_prep(dram_vec, nchunk, dst, dst_tok):
            p.dma("sync", bstage[0:nchunk, :], dram_vec.rearrange("(c p) -> c p", p=128),
                  stream="bst", depth=1, reads=("bstage",), writes=("bstage",))
            p.op("tensor", lambda e, n=nchunk: e.transpose(psB[:, 0:n], bstage[0:n, 0:128], ident_f[0:n, 0:n]),
                 reads=("bstage", "idf", PBB[0]), writes=(PBB[0],))
            p.op("vector", lambda e, n=nchunk, dst=dst: e.tensor_copy(dst[:, 0:n], psB[:, 0:n]),
                 reads=(PBB[0],), writes=(dst_tok, PBB[0]))

        vec_prep(bqkv, MQKV, bq_sb, "bq")
        vec_prep(bo, NK, bo_sb, "bo")
        vec_prep(sg, NH, sg_sb, "sg")
        vec_prep(su, NH, su_sb, "su")
        vec_prep(sd, NK, sd_sb, "sd")
        p.op("vector", lambda e: e.tensor_scalar(bq_sb[:, 0:16], bq_sb[:, 0:16], RSQ, None, ALU.mult),
             reads=("bq",), writes=("bq",))

        p.mark("prep")
        build_steps = []
        for i in range(4):
            build_steps.append(("raw", RawG[i * 1408:(i + 1) * 1408, :], wqg[i * 1408:(i + 1) * 1408, :], "rawg", "rawgdone"))
            build_steps.append(("raw", RawU[i * 1408:(i + 1) * 1408, :], wqu[i * 1408:(i + 1) * 1408, :], "rawu", "rawudone"))
            build_steps.append(("raw", RawD[i * 512:(i + 1) * 512, :], wqd[i * 512:(i + 1) * 512, :], "rawd", "rawddone"))
        for st in range(MQKV):
            build_steps.append(("strip", Wqc, wqkv, st, f"wqc{st // 8}", f"wqcd{st // 8}"))
        for st in range(NK):
            build_steps.append(("strip", Woc, wo, st, f"woc{st // 8}", f"wocd{st // 8}"))
        build_pos = [0]

        def run_builds(n):
            for _ in range(n):
                if build_pos[0] < len(build_steps):
                    step = build_steps[build_pos[0]]
                    build_pos[0] += 1
                    if step[0] == "raw":
                        _, dst, src, strm, tok = step
                        p.dma("gpsimd", dst, src, stream=strm, batch=True, writes=(tok,))
                    else:
                        _, cache, src, st, strm, tok = step
                        p.dma("gpsimd", cache[st], src[:, st * 128:(st + 1) * 128].rearrange("(k p) m -> p k m", p=128),
                              stream=strm, batch=True, writes=(tok,))

        pending_dma = []

        def flush_pending():
            for fn in pending_dma:
                fn()
            pending_dma.clear()

        conv_steps = []

        def conv_load(idx, extra=()):
            wq_ap, cache4, k, r0, rr_n, tag = conv_steps[idx]
            par = idx % 2
            p.dma("sync" if par == 0 else "gpsimd", cvw[par][:, 0:rr_n, :],
                  wq_ap[r0 * 128:(r0 + rr_n) * 128, k * 128:(k + 1) * 128].rearrange("(r p) c -> p r c", p=128),
                  stream="cvi", slot=par, reads=(f"cvw{par}", f"raw{tag}done") + tuple(extra), writes=(f"cvw{par}",))

        def conv_xpose_batch(idx, b, bank):
            wq_ap, cache4, k, r0, rr_n, tag = conv_steps[idx]
            par = idx % 2
            nev = (rr_n + 3) // 4
            b0, b1 = b * 4, min(b * 4 + 4, rr_n)
            for rr in range(b0, b1):
                p.op("tensor", lambda e, rr=rr, bank=bank, par=par: e.transpose(
                    psC[:, bank * 1024 + (rr % 8) * 128: bank * 1024 + (rr % 8) * 128 + 128],
                    cvw[par][:, rr, :], ident_bf[:]),
                    reads=(f"cvw{par}", "idbf", PBC[bank]), writes=(PBC[bank],))
            p.op("vector", lambda e, par=par, b0=b0, b1=b1, bank=bank: e.tensor_copy(
                cvt[par][:, b0 * 128:b1 * 128],
                psC[:, bank * 1024 + (b0 % 8) * 128: bank * 1024 + (b0 % 8) * 128 + (b1 - b0) * 128]),
                reads=(PBC[bank], f"cvt{par}"), writes=(f"cvt{par}", PBC[bank]))
            if b == nev - 1:
                p.dma("gpsimd", cache4[r0:r0 + rr_n, :, k, :].rearrange("r p m -> p r m"),
                      cvt[par][:, 0:rr_n * 128].rearrange("p (r m) -> p r m", m=128),
                      stream=f"{tag}cache", batch=True, reads=(f"cvt{par}",), writes=(f"{tag}done",))

        for wq_ap, cache4, nk, nr, tag in ((RawG, Gc, NK, NH, "g"), (RawU, Uc, NK, NH, "u"), (RawD, Dc, NH, NK, "d")):
            for k in range(nk):
                for r0 in range(0, nr, 11):
                    rr_n = min(11, nr - r0)
                    conv_steps.append((wq_ap, cache4, k, r0, rr_n, tag))
        conv_units = []
        for idx, st in enumerate(conv_steps):
            for b in range((st[4] + 3) // 4):
                conv_units.append((idx, b))
        conv_lp = [0]
        conv_up = [0]

        def run_conv(n, extra=()):
            for _ in range(n):
                if conv_up[0] >= len(conv_units):
                    return
                idx, b = conv_units[conv_up[0]]
                while conv_lp[0] < len(conv_steps) and conv_lp[0] <= idx + 1:
                    conv_load(conv_lp[0], extra)
                    conv_lp[0] += 1
                conv_xpose_batch(idx, b, conv_up[0] % 2)
                conv_up[0] += 1

        CVD = ("gdone", "udone", "ddone")

        p.mark("conv")
        def ln_rstd(src_ap, src_toks, par):
            for i in range(4):
                p.op("vector", lambda e, i=i, par=par, src_ap=src_ap: e.bn_stats(
                    stat[par][:, i * 6:(i + 1) * 6], src_ap[:, i * 512:(i + 1) * 512]),
                    reads=src_toks, writes=(f"stat{par}",))
            p.op("vector", lambda e, par=par: e.bn_aggr(mv[par][:], stat[par][:]),
                 reads=(f"stat{par}",), writes=(f"mv{par}",))
            p.op("scalar", lambda e, par=par: e.activation(rstd[par][:], mv[par][:, 1:2], AF.Ln, bias=EPS),
                 reads=(f"mv{par}", "epsc"), writes=(f"rstd{par}",))
            p.op("scalar", lambda e, par=par: e.activation(rstd[par][:], rstd[par][:], AF.Exp, scale=-0.5),
                 reads=(f"rstd{par}",), writes=(f"rstd{par}",))

        def transpose_to_fm(src_bf_ap, src_toks, n_tiles, dst_fn, dst_tokens, extra=()):
            nev = (n_tiles + 3) // 4
            for b in range(nev):
                r0, r1 = b * 4, min(b * 4 + 4, n_tiles)
                bank = b % 2
                for rr in range(r0, r1):
                    p.op("tensor", lambda e, rr=rr, bank=bank, src_bf_ap=src_bf_ap: e.transpose(
                        psC[:, bank * 1024 + (rr % 8) * 128: bank * 1024 + (rr % 8) * 128 + 128],
                        src_bf_ap[:, rr * 128:(rr + 1) * 128], ident_bf[:]),
                        reads=tuple(src_toks) + ("idbf", PBC[bank]), writes=(PBC[bank],))
                dst_ap = dst_fn(r0, r1)
                p.op("scalar", lambda e, dst_ap=dst_ap, r0=r0, r1=r1, bank=bank: e.activation(
                    dst_ap, psC[:, bank * 1024 + (r0 % 8) * 128: bank * 1024 + (r0 % 8) * 128 + (r1 - r0) * 128]
                    .rearrange("p (k t) -> p k t", t=128),
                    AF.Identity, scale=1.0),
                    reads=(PBC[bank],) + tuple(extra), writes=tuple(dst_tokens) + (PBC[bank],))

        for seq in range(NSEQ):
            pef = () if seq == 0 else (f"PEF_s{seq-1}",)
            pout = () if seq == 0 else (f"OUT_{seq-1}_15",)
            lnpair = (lnst, lnstA1)
            for tc in range(NTC):
                par = tc % 2
                p.dma("sync", x_st[par][:], xin[seq, tc * 128:(tc + 1) * 128, :],
                      stream="xst", slot=par, reads=(f"xst{par}",), writes=(f"xst{par}",))
                ln_rstd(x_st[par][:], (f"xst{par}",), par)
                p.op("vector", lambda e, par=par: e.tensor_scalar(
                    lnpair[par][:], x_st[par][:], mv[par][:, 0:1], rstd[par][:], ALU.subtract, ALU.mult),
                    reads=(f"xst{par}", f"rstd{par}", f"mv{par}", f"lnp{par}"), writes=(f"lnp{par}",))
                transpose_to_fm(lnpair[par][:], (f"lnp{par}",), 16,
                                lambda r0, r1, tc=tc: ln1_fm[:, r0:r1, tc * 128:(tc + 1) * 128],
                                (f"l1f_{tc}",))
                if seq == 0:
                    run_builds(5)
            p.fence("tensor", f"LN1F_s{seq}")
            p.mark(f"s{seq}_ln1")
            for m in range(MQKV):
                par = m % 2
                p.dma("sync", wq_sb[par][:], Wqc[m],
                      stream="wq", slot=par, reads=(f"wqs{par}", f"wqcd{m // 8}") + pef, writes=(f"wqs{par}",))
                if seq == 0:
                    run_conv(2)
                for tt in range(4):
                    bank = (m * 4 + tt) % 4
                    tts = tuple(f"l1f_{tt * 4 + i}" for i in range(4))
                    for k in range(NK):
                        p.op("tensor", lambda e, k=k, tt=tt, par=par, bank=bank: e.matmul(
                            psA[:, bank * 512:(bank + 1) * 512],
                            wq_sb[par][:, k, :], ln1_fm[:, k, tt * 512:(tt + 1) * 512],
                            start=(k == 0), stop=(k == NK - 1)),
                            reads=(f"wqs{par}",) + tts + (PBA[bank],), writes=(PBA[bank],))
                    if m < 16:
                        p.op("scalar", lambda e, m=m, tt=tt, bank=bank, par=par: e.activation(
                            qstage[par][:, tt * 512:(tt + 1) * 512], psA[:, bank * 512:(bank + 1) * 512],
                            AF.Identity, bias=bq_sb[:, m:m + 1], scale=RSQ),
                            reads=(PBA[bank], "bq", f"qstg{par}") + pout,
                            writes=(f"qstg{par}_{tt}", f"qstg{par}", PBA[bank]))
                    elif m < 32:
                        h = m - 16
                        p.op("scalar", lambda e, h=h, tt=tt, bank=bank: e.activation(
                            K_sb[:, h, tt * 512:(tt + 1) * 512], psA[:, bank * 512:(bank + 1) * 512],
                            AF.Identity, bias=bq_sb[:, 16 + h:17 + h], scale=1.0),
                            reads=(PBA[bank], "bq"), writes=(f"K_{h}_{tt}", PBA[bank]))
                    else:
                        h = m - 32
                        vpar = tt % 2
                        p.op("scalar", lambda e, h=h, bank=bank, vpar=vpar: e.activation(
                            vstage[vpar][:], psA[:, bank * 512:(bank + 1) * 512],
                            AF.Identity, bias=bq_sb[:, 32 + h:33 + h], scale=1.0),
                            reads=(PBA[bank], "bq", f"vstg{vpar}") + pout, writes=(f"vstg{vpar}", PBA[bank]))
                        transpose_to_fm(vstage[vpar][:], (f"vstg{vpar}",), 4,
                                        lambda r0, r1, vpar=vpar: vtstage[vpar][:, r0 * 128:r1 * 128].rearrange("p (k t) -> p k t", t=128),
                                        (f"vtst{vpar}",), extra=pout)
                        p.dma("gpsimd", Vs[h, :, tt * 512:(tt + 1) * 512], vtstage[vpar][:],
                              stream="vspill", batch=True, reads=(f"vtst{vpar}",), writes=(f"Vs_{h}_{tt}",))
                if m < 16:
                    p.dma("gpsimd", Qs[m, :, :], qstage[par][:],
                          stream="qspill", batch=True,
                          reads=(f"qstg{par}_0", f"qstg{par}_1", f"qstg{par}_2", f"qstg{par}_3"),
                          writes=(f"Qs_{m}",))
            p.fence("tensor", f"QKVF_s{seq}")
            p.mark(f"s{seq}_qkv")
            for g in range(4):
                def wo_load(m, seq=seq):
                    par = m % 2
                    p.dma("sync", wo_sb[par][:], Woc[m],
                          stream="wos", slot=par,
                          reads=(f"wos{par}", f"wocd{m // 8}", f"LN1F_s{seq}") + CVD, writes=(f"wos{par}",))
                wo_load(0)
                wo_load(1)

                def vq_load(h, g=g, seq=seq):
                    vpar = h % 2
                    p.dma("sync", V_sb[vpar][:, 0:(g + 1) * 4, :],
                          Vs[h, :, 0:(g + 1) * 512].rearrange("p (c t) -> p c t", t=128),
                          stream="vsb", slot=vpar,
                          reads=(f"vsb{vpar}", f"QKVF_s{seq}") + CVD + tuple(f"Vs_{h}_{t}" for t in range(g + 1)),
                          writes=(f"vsb{vpar}",))
                    p.dma("sync", Q_sb[vpar][:], Qs[h, :, g * 512:(g + 1) * 512],
                          stream="qsb", slot=vpar, reads=(f"qsb{vpar}", f"Qs_{h}") + CVD, writes=(f"qsb{vpar}",))

                vq_load(0)
                for h in range(NHEAD):
                    vpar = h % 2
                    if h == 2:
                        flush_pending()
                    if h + 1 < NHEAD:
                        vq_load(h + 1)
                    if seq == 0:
                        run_conv(2)
                    pvb = PBB[g % 2]
                    pvo = PBB[1 - g % 2]

                    def pt_pv(ql, vpar=vpar, g=g):
                        qt = g * 4 + ql
                        pp = ql % 2
                        nsc = (qt + 4) // 4
                        for sc in range(nsc):
                            r0, r1 = sc * 4, min(sc * 4 + 4, qt + 1)
                            bank = sc % 2
                            for rr in range(r0, r1):
                                p.op("tensor", lambda e, rr=rr, bank=bank, pp=pp: e.transpose(
                                    psC[:, bank * 1024 + (rr % 8) * 128: bank * 1024 + (rr % 8) * 128 + 128],
                                    P_sb[pp][:, rr * 128:(rr + 1) * 128], ident_bf[:]),
                                    reads=(f"P{pp}_{sc}", "idbf", PBC[bank]), writes=(PBC[bank],))
                            eng = "vector" if (sc % 2 == 0) else "scalar"
                            psrc = psC[:, bank * 1024 + (r0 % 8) * 128: bank * 1024 + (r0 % 8) * 128 + (r1 - r0) * 128]
                            if eng == "vector":
                                fn = (lambda e, r0=r0, r1=r1, psrc=psrc: e.tensor_copy(
                                    PT_sb[:, r0:r1, :], psrc.rearrange("p (c t) -> p c t", t=128)))
                            else:
                                fn = (lambda e, r0=r0, r1=r1, psrc=psrc: e.activation(
                                    PT_sb[:, r0:r1, :], psrc.rearrange("p (c t) -> p c t", t=128),
                                    AF.Identity, scale=1.0))
                            p.op(eng, fn, reads=(PBC[bank], f"PTb{sc}"), writes=(f"PTb{sc}", PBC[bank]))
                        p.op("vector", lambda e, nsc=nsc, pp=pp: e.reduce_sum(
                            z_sb[:], z4[:, 4 * pp:4 * pp + nsc], axis=AX.X),
                            reads=(f"z4{pp}",), writes=("z",))
                        p.op("vector", lambda e, ql=ql: e.reciprocal(zi4[:, ql:ql + 1], z_sb[:]),
                             reads=("z", "zi4"), writes=("zi4",))
                        for sc2 in range(qt + 1):
                            p.op("tensor", lambda e, sc2=sc2, vpar=vpar, ql=ql, g=g, qt=qt: e.matmul(
                                psB[:, (g % 2) * 512 + ql * 128:(g % 2) * 512 + (ql + 1) * 128],
                                V_sb[vpar][:, sc2, :], PT_sb[:, sc2, :],
                                start=(sc2 == 0), stop=(sc2 == qt)),
                                reads=(f"vsb{vpar}", f"PTb{sc2 // 4}", pvb), writes=(pvb,))

                    pend = None
                    for ql in range(4):
                        qt = g * 4 + ql
                        L = (qt + 1) * 128
                        nsc = (L + 511) // 512
                        pp = ql % 2
                        for sc in range(nsc):
                            wdt = min(512, L - sc * 512)
                            p.op("tensor", lambda e, h=h, sc=sc, wdt=wdt, vpar=vpar, ql=ql: e.matmul(
                                psA[:, sc * 512: sc * 512 + wdt],
                                Q_sb[vpar][:, ql * 128:(ql + 1) * 128], K_sb[:, h, sc * 512: sc * 512 + wdt],
                                start=True, stop=True),
                                reads=(f"qsb{vpar}", f"K_{h}_{sc}", PBA[sc]), writes=(PBA[sc],))
                            if sc == nsc - 1:
                                p.op("vector", lambda e, L=L: e.tensor_add(psA[:, L - 128:L], psA[:, L - 128:L], mask_sb[:]),
                                     reads=("mask", PBA[sc]), writes=(PBA[sc],))
                            p.op("scalar", lambda e, sc=sc, wdt=wdt, pp=pp: e.activation(
                                P_sb[pp][:, sc * 512: sc * 512 + wdt], psA[:, sc * 512: sc * 512 + wdt], AF.Exp,
                                bias=csh_sb[:], scale=1.0, accum_out=z4[:, 4 * pp + sc:4 * pp + sc + 1]),
                                reads=(PBA[sc], "csh", f"P{pp}_{sc}", f"z4{pp}") + CVD,
                                writes=(f"P{pp}_{sc}", f"z4{pp}", PBA[sc]))
                        if pend is not None:
                            pt_pv(pend)
                        pend = ql
                    pt_pv(pend)
                    for ql in range(4):
                        p.op("tensor", lambda e, ql=ql, g=g: e.transpose(
                            psB[0:1, (1 - g % 2) * 512 + ql * 128:(1 - g % 2) * 512 + (ql + 1) * 128],
                            zi4[:, ql:ql + 1], ident_f[:]),
                            reads=("zi4", "idf", pvo), writes=(pvo,))
                    p.op("vector", lambda e, g=g: e.tensor_copy(
                        zrow_bf[:], psB[0:1, (1 - g % 2) * 512:(1 - g % 2) * 512 + 512]),
                        reads=(pvo, "zrow"), writes=("zrow", pvo))
                    p.op("tensor", lambda e, g=g: e.matmul(
                        psB[:, (1 - g % 2) * 512:(1 - g % 2) * 512 + 512], ones_bf[:], zrow_bf[:],
                        start=True, stop=True),
                        reads=("ones", "zrow", pvo), writes=(pvo,))
                    p.op("vector", lambda e, g=g: e.tensor_copy(
                        zbc_sb[:], psB[:, (1 - g % 2) * 512:(1 - g % 2) * 512 + 512]),
                        reads=(pvo, "zbc"), writes=("zbc", pvo))
                    p.op("vector", lambda e, h=h, g=g: e.tensor_tensor(
                        att_sb[:, h, :], psB[:, (g % 2) * 512:(g % 2) * 512 + 512], zbc_sb[:], ALU.mult),
                        reads=(pvb, "zbc"), writes=(f"att_{h}", pvb))
                for m in range(NK):
                    par = m % 2
                    bank = PBA[m % 2]
                    pend_wo = m + 2 if m + 2 < NK else None
                    for k in range(NK):
                        p.op("tensor", lambda e, k=k, par=par, m=m: e.matmul(
                            psA[:, (m % 2) * 512:(m % 2) * 512 + 512], wo_sb[par][:, k, :], att_sb[:, k, :],
                            start=(k == 0), stop=(k == NK - 1)),
                            reads=(f"wos{par}", f"att_{k}", bank), writes=(bank,))
                    if pend_wo is not None:
                        wo_load(pend_wo)
                    p.op("scalar", lambda e, m=m, par=par: e.activation(
                        pfm[par][:], psA[:, (m % 2) * 512:(m % 2) * 512 + 512], AF.Identity,
                        bias=bo_sb[:, m:m + 1], scale=1.0),
                        reads=(bank, "bo", f"pfm{par}") + CVD, writes=(f"pfm{par}", bank))
                    tb = PBB[m % 2]
                    for tcl in range(4):
                        p.op("tensor", lambda e, tcl=tcl, m=m, par=par: e.transpose(
                            psB[:, (m % 2) * 512 + tcl * 128: (m % 2) * 512 + (tcl + 1) * 128],
                            pfm[par][:, tcl * 128:(tcl + 1) * 128], ident_f[:]),
                            reads=(f"pfm{par}", "idf", tb), writes=(tb,))
                    p.op("vector", lambda e, m=m: e.tensor_copy(
                        tmW[:, :, m * 128:(m + 1) * 128],
                        psB[:, (m % 2) * 512:(m % 2) * 512 + 512].rearrange("p (c t) -> p c t", t=128)),
                        reads=(tb, f"tm_{m}"), writes=(f"tm_{m}", tb))
                tmtoks = tuple(f"tm_{m}" for m in range(NK))
                for tcl in range(4):
                    tc = g * 4 + tcl
                    par = tcl % 2
                    p.dma("sync", x_st[par][:], xin[seq, tc * 128:(tc + 1) * 128, :],
                          stream="xst", slot=par, reads=(f"xst{par}",), writes=(f"xst{par}",))
                    srcv = tmW[:, tcl, :]
                    ln_rstd(srcv, tmtoks, par)
                    p.op("vector", lambda e, srcv=srcv, par=par: e.tensor_scalar(
                        srcv, srcv, mv[par][:, 0:1], rstd[par][:], ALU.subtract, ALU.mult),
                        reads=tmtoks + (f"rstd{par}", f"mv{par}"), writes=tmtoks)
                    p.op("vector", lambda e, srcv=srcv, par=par: e.tensor_add(srcv, srcv, x_st[par][:]),
                         reads=(f"xst{par}",) + tmtoks, writes=tmtoks)
                    pending_dma.append(lambda tc=tc, srcv=srcv, tmtoks=tmtoks: p.dma(
                        "gpsimd", X2[tc * 128:(tc + 1) * 128, :], srcv,
                        stream="x2w", batch=True, reads=tmtoks, writes=(f"X2_{tc}",)))
            p.fence("tensor", f"KF_s{seq}")
            p.mark(f"s{seq}_p1d")
            flush_pending()
            if seq == 0:
                run_conv(len(conv_units))
            ln2pair = (lnst, ln2stB1)

            def emit_ln2(g):
                for tcl in range(4):
                    tc = g * 4 + tcl
                    par = tcl % 2
                    p.dma("sync", x_st[par][:], X2[tc * 128:(tc + 1) * 128, :],
                          stream="xst", slot=par, reads=(f"xst{par}", f"X2_{tc}"), writes=(f"xst{par}",))
                    ln_rstd(x_st[par][:], (f"xst{par}",), par)
                    p.op("vector", lambda e, par=par: e.tensor_scalar(
                        ln2pair[par][:], x_st[par][:], mv[par][:, 0:1], rstd[par][:], ALU.subtract, ALU.mult),
                        reads=(f"xst{par}", f"rstd{par}", f"mv{par}", f"l2p{par}"), writes=(f"l2p{par}",))
                    transpose_to_fm(ln2pair[par][:], (f"l2p{par}",), 16,
                                    lambda r0, r1, tcl=tcl: ln2_fm[:, r0:r1, tcl * 128:(tcl + 1) * 128],
                                    (f"l2f_{tcl}",), extra=("X2_15",))
                l2t = tuple(f"l2f_{t}" for t in range(4))
                for k in range(NK):
                    p.op("tensor", lambda e, k=k: e.matmul(
                        psB[0:1, 0:512], ones_col[:], ln2_fm[:, k, :],
                        start=(k == 0), stop=(k == NK - 1)),
                        reads=("onescol", PBB[0]) + l2t, writes=(PBB[0],))
                p.op("scalar", lambda e: e.activation(s2row[:], psB[0:1, 0:512], AF.Identity, scale=1.0),
                     reads=(PBB[0], "s2row"), writes=("s2row", PBB[0]))

            emit_ln2(0)
            for g in range(4):
                l2toks = tuple(f"l2f_{t}" for t in range(4))
                for r in range(NH):
                    par = r % 2
                    if r == 2:
                        flush_pending()
                    p.dma("sync", gc_sb[par][:], Gc[r],
                          stream="gc", slot=par, reads=(f"gcs{par}", "gdone", f"KF_s{seq}"), writes=(f"gcs{par}",))
                    p.dma("sync", uc_sb[par][:], Uc[r],
                          stream="uc", slot=par, reads=(f"ucs{par}", "udone", f"KF_s{seq}"), writes=(f"ucs{par}",))
                    gb = PBA[r % 2]
                    ub = PBA[2 + r % 2]
                    for k in range(NK):
                        p.op("tensor", lambda e, k=k, par=par, r=r: e.matmul(
                            psA[:, (r % 2) * 512:(r % 2) * 512 + 512], gc_sb[par][:, k, :], ln2_fm[:, k, :],
                            start=(k == 0), stop=False),
                            reads=(f"gcs{par}",) + l2toks + (gb,), writes=(gb,))
                    p.op("tensor", lambda e, r=r: e.matmul(
                        psA[:, (r % 2) * 512:(r % 2) * 512 + 512], neg128_row[:], s2row[:],
                        start=False, stop=True),
                        reads=("n128", "s2row", gb), writes=(gb,))
                    for k in range(NK):
                        p.op("tensor", lambda e, k=k, par=par, r=r: e.matmul(
                            psA[:, (2 + r % 2) * 512:(2 + r % 2) * 512 + 512], uc_sb[par][:, k, :], ln2_fm[:, k, :],
                            start=(k == 0), stop=False),
                            reads=(f"ucs{par}",) + l2toks + (ub,), writes=(ub,))
                    p.op("tensor", lambda e, r=r: e.matmul(
                        psA[:, (2 + r % 2) * 512:(2 + r % 2) * 512 + 512], neg128_row[:], s2row[:],
                        start=False, stop=True),
                        reads=("n128", "s2row", ub), writes=(ub,))
                    p.op("scalar", lambda e, r=r, par=par: e.activation(
                        silu_t[par][:], psA[:, (r % 2) * 512:(r % 2) * 512 + 512], AF.Silu, scale=sg_sb[:, r:r + 1]),
                        reads=(gb, "sg", f"silu{par}") + CVD, writes=(f"silu{par}", gb))
                    p.op("vector", lambda e, r=r, par=par: e.scalar_tensor_tensor(
                        h_fm[:, r, :], silu_t[par][:], su_sb[:, r:r + 1],
                        psA[:, (2 + r % 2) * 512:(2 + r % 2) * 512 + 512], ALU.mult, ALU.mult),
                        reads=(f"silu{par}", "su", ub, "X2_15"), writes=(f"h_{r}", ub))
                htoks = tuple(f"h_{r}" for r in range(NH))
                for kh in range(NH):
                    p.op("tensor", lambda e, kh=kh: e.matmul(
                        psB[0:1, 512:1024], ones_col[:], h_fm[:, kh, :],
                        start=(kh == 0), stop=(kh == NH - 1)),
                        reads=("onescol", PBB[1]) + htoks, writes=(PBB[1],))
                p.op("scalar", lambda e: e.activation(s3row_f[:], psB[0:1, 512:1024], AF.Identity, scale=1.0),
                     reads=(PBB[1], "s3rowf"), writes=("s3rowf", PBB[1]))
                p.op("vector", lambda e: e.tensor_copy(s3hi_bf[:], s3row_f[:]),
                     reads=("s3rowf", "s3hibf"), writes=("s3hibf",))
                p.op("vector", lambda e: e.tensor_copy(s3hi_f[:], s3hi_bf[:]),
                     reads=("s3hibf", "s3hif"), writes=("s3hif",))
                p.op("vector", lambda e: e.tensor_sub(s3lo_bf[:], s3row_f[:], s3hi_f[:]),
                     reads=("s3rowf", "s3hif", "s3lobf"), writes=("s3lobf",))
                if g < 3:
                    emit_ln2(g + 1)
                for rd in range(NK):
                    par = rd % 2
                    p.dma("sync", dc_sb[par][:], Dc[rd],
                          stream="dc", slot=par, reads=(f"dcs{par}", "ddone", f"KF_s{seq}"), writes=(f"dcs{par}",))
                    db = PBA[rd % 2]
                    for kh in range(NH):
                        p.op("tensor", lambda e, kh=kh, par=par, rd=rd: e.matmul(
                            psA[:, (rd % 2) * 512:(rd % 2) * 512 + 512], dc_sb[par][:, kh, :], h_fm[:, kh, :],
                            start=(kh == 0), stop=False),
                            reads=(f"dcs{par}",) + htoks + (db,), writes=(db,))
                    p.op("tensor", lambda e, rd=rd: e.matmul(
                        psA[:, (rd % 2) * 512:(rd % 2) * 512 + 512], neg128_row[:], s3hi_bf[:],
                        start=False, stop=False),
                        reads=("n128", "s3hibf", db), writes=(db,))
                    p.op("tensor", lambda e, rd=rd: e.matmul(
                        psA[:, (rd % 2) * 512:(rd % 2) * 512 + 512], neg128_row[:], s3lo_bf[:],
                        start=False, stop=True),
                        reads=("n128", "s3lobf", db), writes=(db,))
                    p.op("vector", lambda e, rd=rd, par=par: e.tensor_scalar(
                        pfm[par][:], psA[:, (rd % 2) * 512:(rd % 2) * 512 + 512], sd_sb[:, rd:rd + 1], None, ALU.mult),
                        reads=(db, "sd", f"pfm{par}"), writes=(f"pfm{par}", db))
                    tb = PBB[rd % 2]
                    for tcl in range(4):
                        p.op("tensor", lambda e, tcl=tcl, rd=rd, par=par: e.transpose(
                            psB[:, (rd % 2) * 512 + tcl * 128: (rd % 2) * 512 + (tcl + 1) * 128],
                            pfm[par][:, tcl * 128:(tcl + 1) * 128], ident_f[:]),
                            reads=(f"pfm{par}", "idf", tb), writes=(tb,))
                    p.op("vector", lambda e, rd=rd: e.tensor_copy(
                        tm_buf[:, :, rd * 128:(rd + 1) * 128],
                        psB[:, (rd % 2) * 512:(rd % 2) * 512 + 512].rearrange("p (c t) -> p c t", t=128)),
                        reads=(tb, f"tf_{rd}"), writes=(f"tf_{rd}", tb, "cvw0", "cvw1", "cvt0", "cvt1"))
                tmtoks = tuple(f"tf_{m}" for m in range(NK))
                for tcl in range(4):
                    tc = g * 4 + tcl
                    par = tcl % 2
                    p.dma("sync", x_st[par][:], X2[tc * 128:(tc + 1) * 128, :],
                          stream="xst", slot=par, reads=(f"xst{par}", f"X2_{tc}"), writes=(f"xst{par}",))
                    srcv = tm_buf[:, tcl, :]
                    ln_rstd(srcv, tmtoks, par)
                    p.op("vector", lambda e, srcv=srcv, par=par: e.tensor_scalar(
                        srcv, srcv, mv[par][:, 0:1], rstd[par][:], ALU.subtract, ALU.mult),
                        reads=tmtoks + (f"rstd{par}", f"mv{par}"), writes=tmtoks)
                    p.op("vector", lambda e, srcv=srcv, par=par: e.tensor_add(srcv, srcv, x_st[par][:]),
                         reads=(f"xst{par}",) + tmtoks, writes=tmtoks)
                    pending_dma.append(lambda tc=tc, srcv=srcv, tmtoks=tmtoks, seq=seq: p.dma(
                        "gpsimd", out[seq, tc * 128:(tc + 1) * 128, :], srcv,
                        stream="outw", batch=True, reads=tmtoks, writes=(f"OUT_{seq}_{tc}",)))
            flush_pending()
            p.fence("tensor", f"PEF_s{seq}")
            p.mark(f"s{seq}_ffn")

        p.op("vector", lambda e: e.drain(),
             reads=tuple(f"OUT_{s}_{t}" for s in range(NSEQ) for t in range(NTC)))

        p.emit(block, ctx, p.marks.get(stop) if stop else None)
    return nc


_CACHE = {}


def kernel(**inputs):
    x = np.ascontiguousarray(np.asarray(inputs["L0_attn_x"], dtype=np.float32))
    wq = np.ascontiguousarray(np.asarray(inputs["L0_attn_Wqkv"], dtype=np.float32))
    bq = np.ascontiguousarray(np.asarray(inputs["L0_attn_bqkv"], dtype=np.float32))
    wo_ = np.ascontiguousarray(np.asarray(inputs["L0_attn_Wo"], dtype=np.float32))
    bo_ = np.ascontiguousarray(np.asarray(inputs["L0_attn_bo"], dtype=np.float32))
    wg = np.ascontiguousarray(np.asarray(inputs["L0_ffn_wq_gate"], dtype=np.int32))
    sg_ = np.ascontiguousarray(np.asarray(inputs["L0_ffn_s_gate"], dtype=np.float32))
    wu = np.ascontiguousarray(np.asarray(inputs["L0_ffn_wq_up"], dtype=np.int32))
    su_ = np.ascontiguousarray(np.asarray(inputs["L0_ffn_s_up"], dtype=np.float32))
    wd = np.ascontiguousarray(np.asarray(inputs["L0_ffn_wq_down"], dtype=np.int32))
    sd_ = np.ascontiguousarray(np.asarray(inputs["L0_ffn_s_down"], dtype=np.float32))

    if "nc" not in _CACHE:
        _CACHE["nc"] = build_nc()
    nc = _CACHE["nc"]

    in_maps = []
    for c in range(8):
        in_maps.append({
            "xin": x[2 * c:2 * c + 2],
            "wqkv": wq, "bqkv": bq, "wo": wo_, "bo": bo_,
            "wqg": wg, "sg": sg_, "wqu": wu, "su": su_, "wqd": wd, "sd": sd_,
        })
    res = run_bass_kernel_spmd(nc, in_maps, core_ids=list(range(8)))
    return np.concatenate([r["out"] for r in res.results], axis=0)


# version 18
# speedup vs baseline: 1.2209x; 1.0247x over previous
import numpy as np
import ml_dtypes
from contextlib import ExitStack

import concourse.bass as bass
import concourse.mybir as mybir
from concourse.bass_utils import run_bass_kernel_spmd

F32 = mybir.dt.float32
BF16 = mybir.dt.bfloat16
I32 = mybir.dt.int32
AF = mybir.ActivationFunctionType
ALU = mybir.AluOpType
AX = mybir.AxisListType

RSQ = 0.08838834764831845
NEGBIG = -1.0e9
EPS = 1e-5

D = 2048
NK = 16
S = 2048
NTC = 16
MQKV = 48
NHEAD = 16
HFF = 5632
NH = 44
NSEQ = 2


class Prog:

    def __init__(self, nc):
        self.nc = nc
        self.ops = []
        self.deps = []
        self.fixed = []
        self.last_writer = {}
        self.readers = {}
        self.marks = {}
        self.stream_cnt = {}
        self.dma_meta = {}

    def mark(self, name):
        self.marks[name] = len(self.ops)

    def fence(self, engine, *tokens):
        j = None
        for k in range(len(self.ops) - 1, -1, -1):
            if self.ops[k][0] == engine and not self.ops[k][2]:
                j = k
                break
        assert j is not None
        for t in tokens:
            self.readers[t] = []
            self.last_writer[t] = j

    def _dep(self, i, j, dep, fx):
        if j in self.dma_meta:
            sem_key, wv, stream, batch = self.dma_meta[j]
            if batch:
                fx.append(("d:" + stream, 16 * self.stream_cnt[stream]))
            else:
                fx.append((sem_key, wv))
        else:
            dep.add(j)

    def op(self, engine, fn, reads=(), writes=(), dma=False):
        i = len(self.ops)
        dep = set()
        fx = []
        for t in reads:
            j = self.last_writer.get(t)
            if j is not None:
                self._dep(i, j, dep, fx)
        for t in writes:
            j = self.last_writer.get(t)
            if j is not None:
                self._dep(i, j, dep, fx)
            for r in self.readers.get(t, ()):
                self._dep(i, r, dep, fx)
        for t in reads:
            self.readers.setdefault(t, []).append(i)
        for t in writes:
            self.readers[t] = []
            self.last_writer[t] = i
        self.ops.append((engine, fn, dma))
        self.deps.append(dep)
        self.fixed.append(fx)
        return i

    def dma(self, q, out, in_, stream, slot=None, batch=False, depth=2, reads=(), writes=()):
        fn = lambda e, out=out, in_=in_: e.dma_start(out=out, in_=in_)
        gen = self.stream_cnt.get(stream, 0)
        if batch:
            sem_key = "d:" + stream
            wv = None
        else:
            if slot is not None:
                assert gen % depth == slot, (stream, gen, slot)
            sem_key = f"d:{stream}:{gen % depth}"
            wv = 16 * (gen // depth + 1)
        i = self.op(q, fn, reads=reads, writes=writes, dma=True)
        self.stream_cnt[stream] = gen + 1
        self.dma_meta[i] = (sem_key, wv, stream, batch)
        return i

    def emit(self, block, ctx, limit=None):
        nc = self.nc
        n = len(self.ops) if limit is None else limit
        pruned = []
        for i in range(n):
            ei = self.ops[i][0]
            dd = {}
            for j in self.deps[i]:
                ej = self.ops[j][0]
                if ej == "tensor" and ei == "tensor" and not self.ops[i][2]:
                    continue
                if dd.get(ej, -1) < j:
                    dd[ej] = j
            pruned.append(dd)
        needed = set()
        for dd in pruned:
            needed.update(dd.values())
        keys = set()
        for i in range(n):
            keys.add(self.ops[i][0])
            if i in self.dma_meta:
                keys.add(self.dma_meta[i][0])
            for (sk, v) in self.fixed[i]:
                keys.add(sk)
        sems = {}
        for s in sorted(keys):
            sems[s] = ctx.enter_context(nc.semaphore("sm_" + s.replace(":", "_")))
        counter = {s: 0 for s in sems}
        val = [None] * n
        for i in range(n):
            if self.ops[i][2]:
                sk = self.dma_meta[i][0]
                counter[sk] += 16
                val[i] = counter[sk]
            elif i in needed:
                s = self.ops[i][0]
                counter[s] += 1
                val[i] = counter[s]
        for s, c in counter.items():
            assert c < 60000, f"sem {s} count {c}"
        waits = [None] * n
        for i in range(n):
            w = {}
            for s, j in pruned[i].items():
                assert val[j] is not None
                w[s] = max(w.get(s, 0), val[j])
            for (sk, v) in self.fixed[i]:
                w[sk] = max(w.get(sk, 0), min(v, counter.get(sk, v)))
            waits[i] = sorted(w.items())

        def mk(engname):
            def inner(eng):
                seen = {}
                for i in range(n):
                    if self.ops[i][0] != engname:
                        continue
                    for (s, v) in waits[i]:
                        if seen.get(s, 0) < v:
                            eng.wait_ge(sems[s], v)
                            seen[s] = v
                    ins = self.ops[i][1](eng)
                    if val[i] is not None:
                        sk = self.dma_meta[i][0] if self.ops[i][2] else self.ops[i][0]
                        ins.then_inc(sems[sk], 16 if self.ops[i][2] else 1)
            return inner

        block.sync(mk("sync"))
        block.gpsimd(mk("gpsimd"))

        def mkv(eng):
            mk("vector")(eng)
            for s in sorted(sems):
                if s.startswith("d:") and counter[s] > 0:
                    eng.wait_ge(sems[s], counter[s])
        block.vector(mkv)
        block.scalar(mk("scalar"))
        block.tensor(mk("tensor"))


def build_nc(stop=None):
    nc = bass.Bass("TRN2", target_bir_lowering=False, debug=False)

    xin = nc.dram_tensor("xin", [NSEQ, S, D], F32, kind="ExternalInput").ap()
    wqkv = nc.dram_tensor("wqkv", [D, 6144], F32, kind="ExternalInput").ap()
    bqkv = nc.dram_tensor("bqkv", [6144], F32, kind="ExternalInput").ap()
    wo = nc.dram_tensor("wo", [D, D], F32, kind="ExternalInput").ap()
    bo = nc.dram_tensor("bo", [D], F32, kind="ExternalInput").ap()
    wqg = nc.dram_tensor("wqg", [HFF, D], I32, kind="ExternalInput").ap()
    sg = nc.dram_tensor("sg", [HFF], F32, kind="ExternalInput").ap()
    wqu = nc.dram_tensor("wqu", [HFF, D], I32, kind="ExternalInput").ap()
    su = nc.dram_tensor("su", [HFF], F32, kind="ExternalInput").ap()
    wqd = nc.dram_tensor("wqd", [D, HFF], I32, kind="ExternalInput").ap()
    sd = nc.dram_tensor("sd", [D], F32, kind="ExternalInput").ap()
    out = nc.dram_tensor("out", [NSEQ, S, D], F32, kind="ExternalOutput").ap()

    Qs = nc.dram_tensor("Qs", [NHEAD, 128, S], BF16, kind="Internal").ap()
    Vs = nc.dram_tensor("Vs", [NHEAD, 128, S], BF16, kind="Internal").ap()
    X2 = nc.dram_tensor("X2", [S, D], F32, kind="Internal").ap()
    Gc = nc.dram_tensor("Gc", [NH, 128, NK, 128], BF16, kind="Internal").ap()
    Uc = nc.dram_tensor("Uc", [NH, 128, NK, 128], BF16, kind="Internal").ap()
    Dc = nc.dram_tensor("Dc", [NK, 128, NH, 128], BF16, kind="Internal").ap()
    Wqc = nc.dram_tensor("Wqc", [MQKV, 128, NK, 128], BF16, kind="Internal").ap()
    Woc = nc.dram_tensor("Woc", [NK, 128, NK, 128], BF16, kind="Internal").ap()
    RawG = nc.dram_tensor("RawG", [HFF, D], BF16, kind="Internal").ap()
    RawU = nc.dram_tensor("RawU", [HFF, D], BF16, kind="Internal").ap()
    RawD = nc.dram_tensor("RawD", [D, HFF], BF16, kind="Internal").ap()

    bfc = lambda a: a.astype(ml_dtypes.bfloat16)
    ident_bf_c = nc.inline_tensor(bfc(np.eye(128, dtype=np.float32)), "idbf").ap()
    ident_f_c = nc.inline_tensor(np.eye(128, dtype=np.float32), "idf").ap()
    mask_c = nc.inline_tensor(np.triu(np.full((128, 128), NEGBIG, dtype=np.float32), 1), "mask").ap()
    ones_c = nc.inline_tensor(np.ones((1, 128), dtype=ml_dtypes.bfloat16), "ones").ap()
    eps_c = nc.inline_tensor(np.full((128, 1), EPS, dtype=np.float32), "eps").ap()
    n128_c = nc.inline_tensor(np.full((1, 128), -128.0, dtype=ml_dtypes.bfloat16), "n128").ap()
    csh_c = nc.inline_tensor(np.full((128, 1), -48.0, dtype=np.float32), "csh").ap()
    onescol_c = nc.inline_tensor(np.ones((128, 1), dtype=ml_dtypes.bfloat16), "onescl").ap()

    S0 = 16512

    def at(name, off_b, shape, dt):
        assert off_b % 32 == 0, (name, off_b)
        return nc.alloc_sbuf_tensor_at(name, shape, dt, offset=off_b)

    RB1 = S0
    RB2 = S0 + 65536
    RC = S0 + 131072
    RD = S0 + 135168
    R4 = S0 + 157696

    ln1_fm = at("ln1fm", RB1, [128, NK, S], BF16)
    att_sb = at("attsb", RB1, [128, NHEAD, 512], BF16)
    h_fm = at("hfm", RB1, [128, NH, 512], BF16)
    ln2_fm = at("ln2fm", RB1 + NH * 1024, [128, NK, 512], BF16)

    K_sb = at("ksb", RB2, [128, NHEAD, S], BF16)
    gc_sb = [at(f"gcs{i}", RB2 + i * 4096, [128, NK, 128], BF16) for i in range(2)]
    uc_sb = [at(f"ucs{i}", RB2 + 8192 + i * 4096, [128, NK, 128], BF16) for i in range(2)]
    dc_sb = [at(f"dcs{i}", RB2 + 16384 + i * 11264, [128, NH, 128], BF16) for i in range(2)]

    PT_sb = at("ptsb", RC, [128, NK, 128], BF16)
    lnst = at("lnst", RC, [128, S], BF16)
    P_sb = [at("psb0", RD, [128, S], BF16), None]
    V_sb = [at(f"vsb{i}", RD + 4096 + i * 4096, [128, NK, 128], BF16) for i in range(2)]
    Q_sb = [at(f"qsb{i}", RD + 12288 + i * 1024, [128, 512], BF16) for i in range(2)]
    wo_sb = [at(f"wos{i}", RD + 14336 + i * 4096, [128, NK, 128], BF16) for i in range(2)]
    wq_sb = [at(f"wqs{i}", RD + i * 4096, [128, NK, 128], BF16) for i in range(2)]
    pfm = [at(f"pfm{i}", RD + i * 2048, [128, 512], F32) for i in range(2)]
    silu_t = [at(f"silu{i}", RD + i * 2048, [128, 512], F32) for i in range(2)]
    cvw = [at(f"cvw{i}", R4 + 12288 + i * 2816, [128, 11, 128], BF16) for i in range(2)]
    cvt = [at(f"cvT{i}", R4 + 17920 + i * 2816, [128, 11 * 128], BF16) for i in range(2)]
    s2row = at("s2row", RD + 4096, [1, 512], BF16)
    s3row_f = at("s3rowf", RD + 5120, [1, 512], F32)
    s3hi_bf = at("s3hibf", RD + 7168, [1, 512], BF16)
    s3hi_f = at("s3hif", RD + 8192, [1, 512], F32)
    s3lo_bf = at("s3lobf", RD + 10240, [1, 512], BF16)
    lnstA1 = at("lnstA1", RD + 14336, [128, S], BF16)
    ln2stB1 = at("ln2stB1", RB1 + 61440, [128, S], BF16)
    tm_buf = at("tmbuf", R4, [128, 4, D], F32)
    x_st = [at(f"xst{i}", R4 + 32768 + i * 8192, [128, D], F32) for i in range(2)]
    qstage = [at(f"qstg{i}", R4 + i * 4096, [128, S], BF16) for i in range(2)]
    vstage = [at(f"vstg{i}", R4 + 8192 + i * 1024, [128, 512], BF16) for i in range(2)]
    vtstage = [at(f"vtst{i}", R4 + 10240 + i * 1024, [128, 512], BF16) for i in range(2)]
    SM = R4 + 49152
    ident_bf = at("identbf", SM, [128, 128], BF16)
    ident_f = at("identf", SM + 256, [128, 128], F32)
    mask_sb = at("masksb", SM + 768, [128, 128], F32)
    bq_sb = at("bqsb", SM + 1280, [128, MQKV], F32)
    bo_sb = at("bosb", SM + 1472, [128, NK], F32)
    sg_sb = at("sgsb", SM + 1536, [128, NH], F32)
    su_sb = at("susb", SM + 1728, [128, NH], F32)
    sd_sb = at("sdsb", SM + 1920, [128, NK], F32)
    bstage = at("bstage", SM + 1984, [128, 128], F32)
    eps_sb = at("epssb", SM + 3392, [128, 1], F32)
    csh_sb = at("cshsb", SM + 3744, [128, 1], F32)
    neg128_row = at("n128r", SM + 3424, [1, 128], BF16)
    ones_col = at("onescol", SM + 3680, [128, 1], BF16)
    stat = [at(f"stat{i}", SM + 2496 + i * 96, [128, 24], F32) for i in range(2)]
    mv = [at(f"mv{i}", SM + 2688 + i * 32, [128, 2], F32) for i in range(2)]
    rstd = [at(f"rstd{i}", SM + 2752 + i * 32, [128, 1], F32) for i in range(2)]
    mrstd = [at(f"mrstd{i}", SM + 2816 + i * 32, [128, 1], F32) for i in range(2)]
    veps = [at(f"veps{i}", SM + 2880 + i * 32, [128, 1], F32) for i in range(2)]
    nmax_sb = at("nmaxsb", SM + 2944, [128, 1], F32)
    z_sb = at("zsb", SM + 2976, [128, 1], F32)
    zi_sb = at("zisb", SM + 3008, [128, 1], F32)
    ones_bf = at("onesbf", SM + 3040, [1, 128], BF16)
    nm4 = at("nm4", SM + 3296, [128, 4], F32)
    z4 = at("z4", SM + 3328, [128, 8], F32)
    zi4 = at("zi4", SM + 3360, [128, 4], F32)
    zrow_bf = at("zrowbf", RB1 + 16384, [1, 512], BF16)
    zbc_sb = at("zbcsb", RB1 + 17408, [128, 512], BF16)
    P_sb[1] = at("psb1", RB1 + 18432, [128, S], BF16)
    tmW = at("tmw", RB1 + 22528, [128, 4, D], F32)

    with ExitStack() as ctx:
        psA = ctx.enter_context(nc.psum_tensor("psA", [128, 2048], F32))
        psB = ctx.enter_context(nc.psum_tensor("psB", [128, 1024], F32))
        psC = ctx.enter_context(nc.psum_tensor("psC", [128, 2048], BF16))
        block = ctx.enter_context(nc.Block())

        p = Prog(nc)
        PBA = [f"pb{i}" for i in range(4)]
        PBB = [f"pb{i + 4}" for i in range(2)]
        PBC = [f"pb{i + 6}" for i in range(2)]

        p.dma("sync", ident_bf[:], ident_bf_c, stream="init", batch=True, writes=("idbf",))
        p.dma("sync", ident_f[:], ident_f_c, stream="init", batch=True, writes=("idf",))
        p.dma("sync", mask_sb[:], mask_c, stream="init", batch=True, writes=("mask",))
        p.dma("sync", ones_bf[:], ones_c, stream="init", batch=True, writes=("ones",))
        nc.const_aps.register(F32, EPS, eps_sb[:])
        p.dma("sync", eps_sb[:], eps_c, stream="init", batch=True, writes=("epsc",))
        p.dma("sync", neg128_row[:], n128_c, stream="init", batch=True, writes=("n128",))
        p.dma("sync", csh_sb[:], csh_c, stream="init", batch=True, writes=("csh",))
        p.dma("sync", ones_col[:], onescol_c, stream="init", batch=True, writes=("onescol",))

        def vec_prep(dram_vec, nchunk, dst, dst_tok):
            p.dma("sync", bstage[0:nchunk, :], dram_vec.rearrange("(c p) -> c p", p=128),
                  stream="bst", depth=1, reads=("bstage",), writes=("bstage",))
            p.op("tensor", lambda e, n=nchunk: e.transpose(psB[:, 0:n], bstage[0:n, 0:128], ident_f[0:n, 0:n]),
                 reads=("bstage", "idf", PBB[0]), writes=(PBB[0],))
            p.op("vector", lambda e, n=nchunk, dst=dst: e.tensor_copy(dst[:, 0:n], psB[:, 0:n]),
                 reads=(PBB[0],), writes=(dst_tok, PBB[0]))

        vec_prep(bqkv, MQKV, bq_sb, "bq")
        vec_prep(bo, NK, bo_sb, "bo")
        vec_prep(sg, NH, sg_sb, "sg")
        vec_prep(su, NH, su_sb, "su")
        vec_prep(sd, NK, sd_sb, "sd")
        p.op("vector", lambda e: e.tensor_scalar(bq_sb[:, 0:16], bq_sb[:, 0:16], RSQ, None, ALU.mult),
             reads=("bq",), writes=("bq",))

        p.mark("prep")
        build_steps = []
        for i in range(4):
            build_steps.append(("raw", RawG[i * 1408:(i + 1) * 1408, :], wqg[i * 1408:(i + 1) * 1408, :], "rawg", "rawgdone"))
            build_steps.append(("raw", RawU[i * 1408:(i + 1) * 1408, :], wqu[i * 1408:(i + 1) * 1408, :], "rawu", "rawudone"))
            build_steps.append(("raw", RawD[i * 512:(i + 1) * 512, :], wqd[i * 512:(i + 1) * 512, :], "rawd", "rawddone"))
        for st in range(MQKV):
            build_steps.append(("strip", Wqc, wqkv, st, f"wqc{st // 8}", f"wqcd{st // 8}"))
        for st in range(NK):
            build_steps.append(("strip", Woc, wo, st, f"woc{st // 8}", f"wocd{st // 8}"))
        build_pos = [0]

        def run_builds(n):
            for _ in range(n):
                if build_pos[0] < len(build_steps):
                    step = build_steps[build_pos[0]]
                    build_pos[0] += 1
                    if step[0] == "raw":
                        _, dst, src, strm, tok = step
                        p.dma("gpsimd", dst, src, stream=strm, batch=True, writes=(tok,))
                    else:
                        _, cache, src, st, strm, tok = step
                        p.dma("gpsimd", cache[st], src[:, st * 128:(st + 1) * 128].rearrange("(k p) m -> p k m", p=128),
                              stream=strm, batch=True, writes=(tok,))

        pending_dma = []

        def flush_pending():
            for fn in pending_dma:
                fn()
            pending_dma.clear()

        conv_steps = []

        def conv_load(idx, extra=()):
            wq_ap, cache4, k, r0, rr_n, tag = conv_steps[idx]
            par = idx % 2
            p.dma("sync" if par == 0 else "gpsimd", cvw[par][:, 0:rr_n, :],
                  wq_ap[r0 * 128:(r0 + rr_n) * 128, k * 128:(k + 1) * 128].rearrange("(r p) c -> p r c", p=128),
                  stream="cvi", slot=par, reads=(f"cvw{par}", f"raw{tag}done") + tuple(extra), writes=(f"cvw{par}",))

        def conv_xpose_batch(idx, b, bank):
            wq_ap, cache4, k, r0, rr_n, tag = conv_steps[idx]
            par = idx % 2
            nev = (rr_n + 3) // 4
            b0, b1 = b * 4, min(b * 4 + 4, rr_n)
            for rr in range(b0, b1):
                p.op("tensor", lambda e, rr=rr, bank=bank, par=par: e.transpose(
                    psC[:, bank * 1024 + (rr % 8) * 128: bank * 1024 + (rr % 8) * 128 + 128],
                    cvw[par][:, rr, :], ident_bf[:]),
                    reads=(f"cvw{par}", "idbf", PBC[bank]), writes=(PBC[bank],))
            p.op("vector", lambda e, par=par, b0=b0, b1=b1, bank=bank: e.tensor_copy(
                cvt[par][:, b0 * 128:b1 * 128],
                psC[:, bank * 1024 + (b0 % 8) * 128: bank * 1024 + (b0 % 8) * 128 + (b1 - b0) * 128]),
                reads=(PBC[bank], f"cvt{par}"), writes=(f"cvt{par}", PBC[bank]))
            if b == nev - 1:
                p.dma("gpsimd", cache4[r0:r0 + rr_n, :, k, :].rearrange("r p m -> p r m"),
                      cvt[par][:, 0:rr_n * 128].rearrange("p (r m) -> p r m", m=128),
                      stream=f"{tag}cache", batch=True, reads=(f"cvt{par}",), writes=(f"{tag}done",))

        for wq_ap, cache4, nk, nr, tag in ((RawG, Gc, NK, NH, "g"), (RawU, Uc, NK, NH, "u"), (RawD, Dc, NH, NK, "d")):
            for k in range(nk):
                for r0 in range(0, nr, 11):
                    rr_n = min(11, nr - r0)
                    conv_steps.append((wq_ap, cache4, k, r0, rr_n, tag))
        conv_units = []
        for idx, st in enumerate(conv_steps):
            for b in range((st[4] + 3) // 4):
                conv_units.append((idx, b))
        conv_lp = [0]
        conv_up = [0]

        def run_conv(n, extra=()):
            for _ in range(n):
                if conv_up[0] >= len(conv_units):
                    return
                idx, b = conv_units[conv_up[0]]
                while conv_lp[0] < len(conv_steps) and conv_lp[0] <= idx + 1:
                    conv_load(conv_lp[0], extra)
                    conv_lp[0] += 1
                conv_xpose_batch(idx, b, conv_up[0] % 2)
                conv_up[0] += 1

        CVD = ("gdone", "udone", "ddone")

        p.mark("conv")
        def ln_rstd(src_ap, src_toks, par):
            for i in range(4):
                p.op("vector", lambda e, i=i, par=par, src_ap=src_ap: e.bn_stats(
                    stat[par][:, i * 6:(i + 1) * 6], src_ap[:, i * 512:(i + 1) * 512]),
                    reads=src_toks, writes=(f"stat{par}",))
            p.op("vector", lambda e, par=par: e.bn_aggr(mv[par][:], stat[par][:]),
                 reads=(f"stat{par}",), writes=(f"mv{par}",))
            p.op("scalar", lambda e, par=par: e.activation(rstd[par][:], mv[par][:, 1:2], AF.Ln, bias=EPS),
                 reads=(f"mv{par}", "epsc"), writes=(f"rstd{par}",))
            p.op("scalar", lambda e, par=par: e.activation(rstd[par][:], rstd[par][:], AF.Exp, scale=-0.5),
                 reads=(f"rstd{par}",), writes=(f"rstd{par}",))

        def transpose_to_fm(src_bf_ap, src_toks, n_tiles, dst_fn, dst_tokens, extra=()):
            nev = (n_tiles + 3) // 4
            for b in range(nev):
                r0, r1 = b * 4, min(b * 4 + 4, n_tiles)
                bank = b % 2
                for rr in range(r0, r1):
                    p.op("tensor", lambda e, rr=rr, bank=bank, src_bf_ap=src_bf_ap: e.transpose(
                        psC[:, bank * 1024 + (rr % 8) * 128: bank * 1024 + (rr % 8) * 128 + 128],
                        src_bf_ap[:, rr * 128:(rr + 1) * 128], ident_bf[:]),
                        reads=tuple(src_toks) + ("idbf", PBC[bank]), writes=(PBC[bank],))
                dst_ap = dst_fn(r0, r1)
                p.op("scalar", lambda e, dst_ap=dst_ap, r0=r0, r1=r1, bank=bank: e.activation(
                    dst_ap, psC[:, bank * 1024 + (r0 % 8) * 128: bank * 1024 + (r0 % 8) * 128 + (r1 - r0) * 128]
                    .rearrange("p (k t) -> p k t", t=128),
                    AF.Identity, scale=1.0),
                    reads=(PBC[bank],) + tuple(extra), writes=tuple(dst_tokens) + (PBC[bank],))

        for seq in range(NSEQ):
            pef = () if seq == 0 else (f"PEF_s{seq-1}",)
            pout = () if seq == 0 else (f"OUT_{seq-1}_15",)
            lnpair = (lnst, lnstA1)
            for tc in range(NTC):
                par = tc % 2
                p.dma("sync", x_st[par][:], xin[seq, tc * 128:(tc + 1) * 128, :],
                      stream="xst", slot=par, reads=(f"xst{par}",), writes=(f"xst{par}",))
                ln_rstd(x_st[par][:], (f"xst{par}",), par)
                p.op("vector", lambda e, par=par: e.tensor_scalar(
                    lnpair[par][:], x_st[par][:], mv[par][:, 0:1], rstd[par][:], ALU.subtract, ALU.mult),
                    reads=(f"xst{par}", f"rstd{par}", f"mv{par}", f"lnp{par}"), writes=(f"lnp{par}",))
                transpose_to_fm(lnpair[par][:], (f"lnp{par}",), 16,
                                lambda r0, r1, tc=tc: ln1_fm[:, r0:r1, tc * 128:(tc + 1) * 128],
                                (f"l1f_{tc}",))
                if seq == 0:
                    run_builds(5)
            p.fence("tensor", f"LN1F_s{seq}")
            p.mark(f"s{seq}_ln1")
            for m in range(MQKV):
                par = m % 2
                p.dma("sync", wq_sb[par][:], Wqc[m],
                      stream="wq", slot=par, reads=(f"wqs{par}", f"wqcd{m // 8}") + pef, writes=(f"wqs{par}",))
                if seq == 0:
                    run_conv(3)
                for tt in range(4):
                    bank = (m * 4 + tt) % 4
                    tts = tuple(f"l1f_{tt * 4 + i}" for i in range(4))
                    for k in range(NK):
                        p.op("tensor", lambda e, k=k, tt=tt, par=par, bank=bank: e.matmul(
                            psA[:, bank * 512:(bank + 1) * 512],
                            wq_sb[par][:, k, :], ln1_fm[:, k, tt * 512:(tt + 1) * 512],
                            start=(k == 0), stop=(k == NK - 1)),
                            reads=(f"wqs{par}",) + tts + (PBA[bank],), writes=(PBA[bank],))
                    if m < 16:
                        p.op("scalar", lambda e, m=m, tt=tt, bank=bank, par=par: e.activation(
                            qstage[par][:, tt * 512:(tt + 1) * 512], psA[:, bank * 512:(bank + 1) * 512],
                            AF.Identity, bias=bq_sb[:, m:m + 1], scale=RSQ),
                            reads=(PBA[bank], "bq", f"qstg{par}") + pout,
                            writes=(f"qstg{par}_{tt}", f"qstg{par}", PBA[bank]))
                    elif m < 32:
                        h = m - 16
                        p.op("scalar", lambda e, h=h, tt=tt, bank=bank: e.activation(
                            K_sb[:, h, tt * 512:(tt + 1) * 512], psA[:, bank * 512:(bank + 1) * 512],
                            AF.Identity, bias=bq_sb[:, 16 + h:17 + h], scale=1.0),
                            reads=(PBA[bank], "bq"), writes=(f"K_{h}_{tt}", PBA[bank]))
                    else:
                        h = m - 32
                        vpar = tt % 2
                        p.op("scalar", lambda e, h=h, bank=bank, vpar=vpar: e.activation(
                            vstage[vpar][:], psA[:, bank * 512:(bank + 1) * 512],
                            AF.Identity, bias=bq_sb[:, 32 + h:33 + h], scale=1.0),
                            reads=(PBA[bank], "bq", f"vstg{vpar}") + pout, writes=(f"vstg{vpar}", PBA[bank]))
                        transpose_to_fm(vstage[vpar][:], (f"vstg{vpar}",), 4,
                                        lambda r0, r1, vpar=vpar: vtstage[vpar][:, r0 * 128:r1 * 128].rearrange("p (k t) -> p k t", t=128),
                                        (f"vtst{vpar}",), extra=pout)
                        p.dma("gpsimd", Vs[h, :, tt * 512:(tt + 1) * 512], vtstage[vpar][:],
                              stream="vspill", batch=True, reads=(f"vtst{vpar}",), writes=(f"Vs_{h}_{tt}",))
                if m < 16:
                    p.dma("gpsimd", Qs[m, :, :], qstage[par][:],
                          stream="qspill", batch=True,
                          reads=(f"qstg{par}_0", f"qstg{par}_1", f"qstg{par}_2", f"qstg{par}_3"),
                          writes=(f"Qs_{m}",))
            p.fence("tensor", f"QKVF_s{seq}")
            p.mark(f"s{seq}_qkv")
            for g in range(4):
                def wo_load(m, seq=seq):
                    par = m % 2
                    p.dma("sync", wo_sb[par][:], Woc[m],
                          stream="wos", slot=par,
                          reads=(f"wos{par}", f"wocd{m // 8}", f"LN1F_s{seq}") + CVD, writes=(f"wos{par}",))
                wo_load(0)
                wo_load(1)

                def vq_load(h, g=g, seq=seq):
                    vpar = h % 2
                    p.dma("sync", V_sb[vpar][:, 0:(g + 1) * 4, :],
                          Vs[h, :, 0:(g + 1) * 512].rearrange("p (c t) -> p c t", t=128),
                          stream="vsb", slot=vpar,
                          reads=(f"vsb{vpar}", f"QKVF_s{seq}") + CVD + tuple(f"Vs_{h}_{t}" for t in range(g + 1)),
                          writes=(f"vsb{vpar}",))
                    p.dma("sync", Q_sb[vpar][:], Qs[h, :, g * 512:(g + 1) * 512],
                          stream="qsb", slot=vpar, reads=(f"qsb{vpar}", f"Qs_{h}") + CVD, writes=(f"qsb{vpar}",))

                vq_load(0)
                for h in range(NHEAD):
                    vpar = h % 2
                    if h == 2:
                        flush_pending()
                    if h + 1 < NHEAD:
                        vq_load(h + 1)
                    pvb = PBB[g % 2]
                    pvo = PBB[1 - g % 2]

                    def pt_pv(ql, vpar=vpar, g=g):
                        qt = g * 4 + ql
                        pp = ql % 2
                        nsc = (qt + 4) // 4
                        for sc in range(nsc):
                            r0, r1 = sc * 4, min(sc * 4 + 4, qt + 1)
                            bank = sc % 2
                            for rr in range(r0, r1):
                                p.op("tensor", lambda e, rr=rr, bank=bank, pp=pp: e.transpose(
                                    psC[:, bank * 1024 + (rr % 8) * 128: bank * 1024 + (rr % 8) * 128 + 128],
                                    P_sb[pp][:, rr * 128:(rr + 1) * 128], ident_bf[:]),
                                    reads=(f"P{pp}_{sc}", "idbf", PBC[bank]), writes=(PBC[bank],))
                            eng = "vector" if (sc % 2 == 0) else "scalar"
                            psrc = psC[:, bank * 1024 + (r0 % 8) * 128: bank * 1024 + (r0 % 8) * 128 + (r1 - r0) * 128]
                            if eng == "vector":
                                fn = (lambda e, r0=r0, r1=r1, psrc=psrc: e.tensor_copy(
                                    PT_sb[:, r0:r1, :], psrc.rearrange("p (c t) -> p c t", t=128)))
                            else:
                                fn = (lambda e, r0=r0, r1=r1, psrc=psrc: e.activation(
                                    PT_sb[:, r0:r1, :], psrc.rearrange("p (c t) -> p c t", t=128),
                                    AF.Identity, scale=1.0))
                            p.op(eng, fn, reads=(PBC[bank], f"PTb{sc}"), writes=(f"PTb{sc}", PBC[bank]))
                        p.op("vector", lambda e, nsc=nsc, pp=pp: e.reduce_sum(
                            z_sb[:], z4[:, 4 * pp:4 * pp + nsc], axis=AX.X),
                            reads=(f"z4{pp}",), writes=("z",))
                        p.op("vector", lambda e, ql=ql: e.reciprocal(zi4[:, ql:ql + 1], z_sb[:]),
                             reads=("z", "zi4"), writes=("zi4",))
                        for sc2 in range(qt + 1):
                            p.op("tensor", lambda e, sc2=sc2, vpar=vpar, ql=ql, g=g, qt=qt: e.matmul(
                                psB[:, (g % 2) * 512 + ql * 128:(g % 2) * 512 + (ql + 1) * 128],
                                V_sb[vpar][:, sc2, :], PT_sb[:, sc2, :],
                                start=(sc2 == 0), stop=(sc2 == qt)),
                                reads=(f"vsb{vpar}", f"PTb{sc2 // 4}", pvb), writes=(pvb,))

                    pend = None
                    for ql in range(4):
                        if seq == 0:
                            run_conv(2)
                        qt = g * 4 + ql
                        L = (qt + 1) * 128
                        nsc = (L + 511) // 512
                        pp = ql % 2
                        for sc in range(nsc):
                            wdt = min(512, L - sc * 512)
                            p.op("tensor", lambda e, h=h, sc=sc, wdt=wdt, vpar=vpar, ql=ql: e.matmul(
                                psA[:, sc * 512: sc * 512 + wdt],
                                Q_sb[vpar][:, ql * 128:(ql + 1) * 128], K_sb[:, h, sc * 512: sc * 512 + wdt],
                                start=True, stop=True),
                                reads=(f"qsb{vpar}", f"K_{h}_{sc}", PBA[sc]), writes=(PBA[sc],))
                            if sc == nsc - 1:
                                p.op("vector", lambda e, L=L: e.tensor_add(psA[:, L - 128:L], psA[:, L - 128:L], mask_sb[:]),
                                     reads=("mask", PBA[sc]), writes=(PBA[sc],))
                            p.op("scalar", lambda e, sc=sc, wdt=wdt, pp=pp: e.activation(
                                P_sb[pp][:, sc * 512: sc * 512 + wdt], psA[:, sc * 512: sc * 512 + wdt], AF.Exp,
                                bias=csh_sb[:], scale=1.0, accum_out=z4[:, 4 * pp + sc:4 * pp + sc + 1]),
                                reads=(PBA[sc], "csh", f"P{pp}_{sc}", f"z4{pp}") + CVD,
                                writes=(f"P{pp}_{sc}", f"z4{pp}", PBA[sc]))
                        if pend is not None:
                            pt_pv(pend)
                        pend = ql
                    pt_pv(pend)
                    for ql in range(4):
                        p.op("tensor", lambda e, ql=ql, g=g: e.transpose(
                            psB[0:1, (1 - g % 2) * 512 + ql * 128:(1 - g % 2) * 512 + (ql + 1) * 128],
                            zi4[:, ql:ql + 1], ident_f[:]),
                            reads=("zi4", "idf", pvo), writes=(pvo,))
                    p.op("vector", lambda e, g=g: e.tensor_copy(
                        zrow_bf[:], psB[0:1, (1 - g % 2) * 512:(1 - g % 2) * 512 + 512]),
                        reads=(pvo, "zrow"), writes=("zrow", pvo))
                    p.op("tensor", lambda e, g=g: e.matmul(
                        psB[:, (1 - g % 2) * 512:(1 - g % 2) * 512 + 512], ones_bf[:], zrow_bf[:],
                        start=True, stop=True),
                        reads=("ones", "zrow", pvo), writes=(pvo,))
                    p.op("vector", lambda e, g=g: e.tensor_copy(
                        zbc_sb[:], psB[:, (1 - g % 2) * 512:(1 - g % 2) * 512 + 512]),
                        reads=(pvo, "zbc"), writes=("zbc", pvo))
                    p.op("vector", lambda e, h=h, g=g: e.tensor_tensor(
                        att_sb[:, h, :], psB[:, (g % 2) * 512:(g % 2) * 512 + 512], zbc_sb[:], ALU.mult),
                        reads=(pvb, "zbc"), writes=(f"att_{h}", pvb))
                for m in range(NK):
                    par = m % 2
                    bank = PBA[m % 2]
                    pend_wo = m + 2 if m + 2 < NK else None
                    for k in range(NK):
                        p.op("tensor", lambda e, k=k, par=par, m=m: e.matmul(
                            psA[:, (m % 2) * 512:(m % 2) * 512 + 512], wo_sb[par][:, k, :], att_sb[:, k, :],
                            start=(k == 0), stop=(k == NK - 1)),
                            reads=(f"wos{par}", f"att_{k}", bank), writes=(bank,))
                    if pend_wo is not None:
                        wo_load(pend_wo)
                    p.op("scalar", lambda e, m=m, par=par: e.activation(
                        pfm[par][:], psA[:, (m % 2) * 512:(m % 2) * 512 + 512], AF.Identity,
                        bias=bo_sb[:, m:m + 1], scale=1.0),
                        reads=(bank, "bo", f"pfm{par}") + CVD, writes=(f"pfm{par}", bank))
                    tb = PBB[m % 2]
                    for tcl in range(4):
                        p.op("tensor", lambda e, tcl=tcl, m=m, par=par: e.transpose(
                            psB[:, (m % 2) * 512 + tcl * 128: (m % 2) * 512 + (tcl + 1) * 128],
                            pfm[par][:, tcl * 128:(tcl + 1) * 128], ident_f[:]),
                            reads=(f"pfm{par}", "idf", tb), writes=(tb,))
                    p.op("vector", lambda e, m=m: e.tensor_copy(
                        tmW[:, :, m * 128:(m + 1) * 128],
                        psB[:, (m % 2) * 512:(m % 2) * 512 + 512].rearrange("p (c t) -> p c t", t=128)),
                        reads=(tb, f"tm_{m}"), writes=(f"tm_{m}", tb))
                tmtoks = tuple(f"tm_{m}" for m in range(NK))
                for tcl in range(4):
                    tc = g * 4 + tcl
                    par = tcl % 2
                    p.dma("sync", x_st[par][:], xin[seq, tc * 128:(tc + 1) * 128, :],
                          stream="xst", slot=par, reads=(f"xst{par}",), writes=(f"xst{par}",))
                    srcv = tmW[:, tcl, :]
                    ln_rstd(srcv, tmtoks, par)
                    p.op("vector", lambda e, srcv=srcv, par=par: e.tensor_scalar(
                        srcv, srcv, mv[par][:, 0:1], rstd[par][:], ALU.subtract, ALU.mult),
                        reads=tmtoks + (f"rstd{par}", f"mv{par}"), writes=tmtoks)
                    p.op("vector", lambda e, srcv=srcv, par=par: e.tensor_add(srcv, srcv, x_st[par][:]),
                         reads=(f"xst{par}",) + tmtoks, writes=tmtoks)
                    pending_dma.append(lambda tc=tc, srcv=srcv, tmtoks=tmtoks: p.dma(
                        "gpsimd", X2[tc * 128:(tc + 1) * 128, :], srcv,
                        stream="x2w", batch=True, reads=tmtoks, writes=(f"X2_{tc}",)))
            p.fence("tensor", f"KF_s{seq}")
            p.mark(f"s{seq}_p1d")
            flush_pending()
            if seq == 0:
                run_conv(len(conv_units))
            ln2pair = (lnst, ln2stB1)

            def emit_ln2(g):
                for tcl in range(4):
                    tc = g * 4 + tcl
                    par = tcl % 2
                    p.dma("sync", x_st[par][:], X2[tc * 128:(tc + 1) * 128, :],
                          stream="xst", slot=par, reads=(f"xst{par}", f"X2_{tc}"), writes=(f"xst{par}",))
                    ln_rstd(x_st[par][:], (f"xst{par}",), par)
                    p.op("vector", lambda e, par=par: e.tensor_scalar(
                        ln2pair[par][:], x_st[par][:], mv[par][:, 0:1], rstd[par][:], ALU.subtract, ALU.mult),
                        reads=(f"xst{par}", f"rstd{par}", f"mv{par}", f"l2p{par}"), writes=(f"l2p{par}",))
                    transpose_to_fm(ln2pair[par][:], (f"l2p{par}",), 16,
                                    lambda r0, r1, tcl=tcl: ln2_fm[:, r0:r1, tcl * 128:(tcl + 1) * 128],
                                    (f"l2f_{tcl}",), extra=("X2_15",))
                l2t = tuple(f"l2f_{t}" for t in range(4))
                for k in range(NK):
                    p.op("tensor", lambda e, k=k: e.matmul(
                        psB[0:1, 0:512], ones_col[:], ln2_fm[:, k, :],
                        start=(k == 0), stop=(k == NK - 1)),
                        reads=("onescol", PBB[0]) + l2t, writes=(PBB[0],))
                p.op("scalar", lambda e: e.activation(s2row[:], psB[0:1, 0:512], AF.Identity, scale=1.0),
                     reads=(PBB[0], "s2row"), writes=("s2row", PBB[0]))

            emit_ln2(0)
            for g in range(4):
                l2toks = tuple(f"l2f_{t}" for t in range(4))
                for r in range(NH):
                    par = r % 2
                    if r == 2:
                        flush_pending()
                    p.dma("sync", gc_sb[par][:], Gc[r],
                          stream="gc", slot=par, reads=(f"gcs{par}", "gdone", f"KF_s{seq}"), writes=(f"gcs{par}",))
                    p.dma("sync", uc_sb[par][:], Uc[r],
                          stream="uc", slot=par, reads=(f"ucs{par}", "udone", f"KF_s{seq}"), writes=(f"ucs{par}",))
                    gb = PBA[r % 2]
                    ub = PBA[2 + r % 2]
                    for k in range(NK):
                        p.op("tensor", lambda e, k=k, par=par, r=r: e.matmul(
                            psA[:, (r % 2) * 512:(r % 2) * 512 + 512], gc_sb[par][:, k, :], ln2_fm[:, k, :],
                            start=(k == 0), stop=False),
                            reads=(f"gcs{par}",) + l2toks + (gb,), writes=(gb,))
                    p.op("tensor", lambda e, r=r: e.matmul(
                        psA[:, (r % 2) * 512:(r % 2) * 512 + 512], neg128_row[:], s2row[:],
                        start=False, stop=True),
                        reads=("n128", "s2row", gb), writes=(gb,))
                    for k in range(NK):
                        p.op("tensor", lambda e, k=k, par=par, r=r: e.matmul(
                            psA[:, (2 + r % 2) * 512:(2 + r % 2) * 512 + 512], uc_sb[par][:, k, :], ln2_fm[:, k, :],
                            start=(k == 0), stop=False),
                            reads=(f"ucs{par}",) + l2toks + (ub,), writes=(ub,))
                    p.op("tensor", lambda e, r=r: e.matmul(
                        psA[:, (2 + r % 2) * 512:(2 + r % 2) * 512 + 512], neg128_row[:], s2row[:],
                        start=False, stop=True),
                        reads=("n128", "s2row", ub), writes=(ub,))
                    p.op("scalar", lambda e, r=r, par=par: e.activation(
                        silu_t[par][:], psA[:, (r % 2) * 512:(r % 2) * 512 + 512], AF.Silu, scale=sg_sb[:, r:r + 1]),
                        reads=(gb, "sg", f"silu{par}") + CVD, writes=(f"silu{par}", gb))
                    p.op("vector", lambda e, r=r, par=par: e.scalar_tensor_tensor(
                        h_fm[:, r, :], silu_t[par][:], su_sb[:, r:r + 1],
                        psA[:, (2 + r % 2) * 512:(2 + r % 2) * 512 + 512], ALU.mult, ALU.mult),
                        reads=(f"silu{par}", "su", ub, "X2_15"), writes=(f"h_{r}", ub))
                htoks = tuple(f"h_{r}" for r in range(NH))
                for kh in range(NH):
                    p.op("tensor", lambda e, kh=kh: e.matmul(
                        psB[0:1, 512:1024], ones_col[:], h_fm[:, kh, :],
                        start=(kh == 0), stop=(kh == NH - 1)),
                        reads=("onescol", PBB[1]) + htoks, writes=(PBB[1],))
                p.op("scalar", lambda e: e.activation(s3row_f[:], psB[0:1, 512:1024], AF.Identity, scale=1.0),
                     reads=(PBB[1], "s3rowf"), writes=("s3rowf", PBB[1]))
                p.op("vector", lambda e: e.tensor_copy(s3hi_bf[:], s3row_f[:]),
                     reads=("s3rowf", "s3hibf"), writes=("s3hibf",))
                p.op("vector", lambda e: e.tensor_copy(s3hi_f[:], s3hi_bf[:]),
                     reads=("s3hibf", "s3hif"), writes=("s3hif",))
                p.op("vector", lambda e: e.tensor_sub(s3lo_bf[:], s3row_f[:], s3hi_f[:]),
                     reads=("s3rowf", "s3hif", "s3lobf"), writes=("s3lobf",))
                if g < 3:
                    emit_ln2(g + 1)
                for rd in range(NK):
                    par = rd % 2
                    p.dma("sync", dc_sb[par][:], Dc[rd],
                          stream="dc", slot=par, reads=(f"dcs{par}", "ddone", f"KF_s{seq}"), writes=(f"dcs{par}",))
                    db = PBA[rd % 2]
                    for kh in range(NH):
                        p.op("tensor", lambda e, kh=kh, par=par, rd=rd: e.matmul(
                            psA[:, (rd % 2) * 512:(rd % 2) * 512 + 512], dc_sb[par][:, kh, :], h_fm[:, kh, :],
                            start=(kh == 0), stop=False),
                            reads=(f"dcs{par}",) + htoks + (db,), writes=(db,))
                    p.op("tensor", lambda e, rd=rd: e.matmul(
                        psA[:, (rd % 2) * 512:(rd % 2) * 512 + 512], neg128_row[:], s3hi_bf[:],
                        start=False, stop=False),
                        reads=("n128", "s3hibf", db), writes=(db,))
                    p.op("tensor", lambda e, rd=rd: e.matmul(
                        psA[:, (rd % 2) * 512:(rd % 2) * 512 + 512], neg128_row[:], s3lo_bf[:],
                        start=False, stop=True),
                        reads=("n128", "s3lobf", db), writes=(db,))
                    p.op("vector", lambda e, rd=rd, par=par: e.tensor_scalar(
                        pfm[par][:], psA[:, (rd % 2) * 512:(rd % 2) * 512 + 512], sd_sb[:, rd:rd + 1], None, ALU.mult),
                        reads=(db, "sd", f"pfm{par}"), writes=(f"pfm{par}", db))
                    tb = PBB[rd % 2]
                    for tcl in range(4):
                        p.op("tensor", lambda e, tcl=tcl, rd=rd, par=par: e.transpose(
                            psB[:, (rd % 2) * 512 + tcl * 128: (rd % 2) * 512 + (tcl + 1) * 128],
                            pfm[par][:, tcl * 128:(tcl + 1) * 128], ident_f[:]),
                            reads=(f"pfm{par}", "idf", tb), writes=(tb,))
                    p.op("vector", lambda e, rd=rd: e.tensor_copy(
                        tm_buf[:, :, rd * 128:(rd + 1) * 128],
                        psB[:, (rd % 2) * 512:(rd % 2) * 512 + 512].rearrange("p (c t) -> p c t", t=128)),
                        reads=(tb, f"tf_{rd}"), writes=(f"tf_{rd}", tb, "cvw0", "cvw1", "cvt0", "cvt1"))
                tmtoks = tuple(f"tf_{m}" for m in range(NK))
                for tcl in range(4):
                    tc = g * 4 + tcl
                    par = tcl % 2
                    p.dma("sync", x_st[par][:], X2[tc * 128:(tc + 1) * 128, :],
                          stream="xst", slot=par, reads=(f"xst{par}", f"X2_{tc}"), writes=(f"xst{par}",))
                    srcv = tm_buf[:, tcl, :]
                    ln_rstd(srcv, tmtoks, par)
                    p.op("vector", lambda e, srcv=srcv, par=par: e.tensor_scalar(
                        srcv, srcv, mv[par][:, 0:1], rstd[par][:], ALU.subtract, ALU.mult),
                        reads=tmtoks + (f"rstd{par}", f"mv{par}"), writes=tmtoks)
                    p.op("vector", lambda e, srcv=srcv, par=par: e.tensor_add(srcv, srcv, x_st[par][:]),
                         reads=(f"xst{par}",) + tmtoks, writes=tmtoks)
                    pending_dma.append(lambda tc=tc, srcv=srcv, tmtoks=tmtoks, seq=seq: p.dma(
                        "gpsimd", out[seq, tc * 128:(tc + 1) * 128, :], srcv,
                        stream="outw", batch=True, reads=tmtoks, writes=(f"OUT_{seq}_{tc}",)))
            flush_pending()
            p.fence("tensor", f"PEF_s{seq}")
            p.mark(f"s{seq}_ffn")

        p.op("vector", lambda e: e.drain(),
             reads=tuple(f"OUT_{s}_{t}" for s in range(NSEQ) for t in range(NTC)))

        p.emit(block, ctx, p.marks.get(stop) if stop else None)
    return nc


_CACHE = {}


def kernel(**inputs):
    x = np.ascontiguousarray(np.asarray(inputs["L0_attn_x"], dtype=np.float32))
    wq = np.ascontiguousarray(np.asarray(inputs["L0_attn_Wqkv"], dtype=np.float32))
    bq = np.ascontiguousarray(np.asarray(inputs["L0_attn_bqkv"], dtype=np.float32))
    wo_ = np.ascontiguousarray(np.asarray(inputs["L0_attn_Wo"], dtype=np.float32))
    bo_ = np.ascontiguousarray(np.asarray(inputs["L0_attn_bo"], dtype=np.float32))
    wg = np.ascontiguousarray(np.asarray(inputs["L0_ffn_wq_gate"], dtype=np.int32))
    sg_ = np.ascontiguousarray(np.asarray(inputs["L0_ffn_s_gate"], dtype=np.float32))
    wu = np.ascontiguousarray(np.asarray(inputs["L0_ffn_wq_up"], dtype=np.int32))
    su_ = np.ascontiguousarray(np.asarray(inputs["L0_ffn_s_up"], dtype=np.float32))
    wd = np.ascontiguousarray(np.asarray(inputs["L0_ffn_wq_down"], dtype=np.int32))
    sd_ = np.ascontiguousarray(np.asarray(inputs["L0_ffn_s_down"], dtype=np.float32))

    if "nc" not in _CACHE:
        _CACHE["nc"] = build_nc()
    nc = _CACHE["nc"]

    in_maps = []
    for c in range(8):
        in_maps.append({
            "xin": x[2 * c:2 * c + 2],
            "wqkv": wq, "bqkv": bq, "wo": wo_, "bo": bo_,
            "wqg": wg, "sg": sg_, "wqu": wu, "su": su_, "wqd": wd, "sd": sd_,
        })
    res = run_bass_kernel_spmd(nc, in_maps, core_ids=list(range(8)))
    return np.concatenate([r["out"] for r in res.results], axis=0)
